# version 10
# baseline (speedup 1.0000x reference)
import numpy as np

import concourse.bass as bass
import concourse.mybir as mybir
from concourse.bass_utils import run_bass_kernel_spmd

F32 = mybir.dt.float32
BF16 = mybir.dt.bfloat16
AF = mybir.ActivationFunctionType
AL = mybir.AluOpType
AX = mybir.AxisListType

T = 2048
SB = 512
NSB = T // SB
NT = T // 128
NCH = 16
NFI = 8
NIO = 32
D = 1024
DI = 2048
EPS = 1e-5

HW = 64
TH = T + HW
R_DTM = 0
R_SUMBP = TH
R_XM = 2 * TH
BLOCKS = [(64, 0), (512, 64), (512, 576), (512, 1088), (512, 1600)]
RB_MC = 0
RB_X = 512
RB_Y = 1024
RB_Z = 1536


class _StopEmit(Exception):
    pass


def build_nc(debug=False, upto="end"):
    nc = bass.Bass("TRN2", target_bir_lowering=False)

    xin = nc.dram_tensor("xin", [T, D], F32, kind="ExternalInput").ap()
    xhalo = nc.dram_tensor("xhalo", [66, D], F32, kind="ExternalInput").ap()
    w_in = nc.dram_tensor("w_in", [D, 2 * DI], F32, kind="ExternalInput").ap()
    w_conv = nc.dram_tensor("w_conv", [128, NCH * 3], F32, kind="ExternalInput").ap()
    w_xproj = nc.dram_tensor("w_xproj", [DI, 96], F32, kind="ExternalInput").ap()
    w_dt = nc.dram_tensor("w_dt", [64, DI], F32, kind="ExternalInput").ap()
    b_dt = nc.dram_tensor("b_dt", [128, NCH], F32, kind="ExternalInput").ap()
    w_outp = nc.dram_tensor("w_outp", [DI, D], F32, kind="ExternalInput").ap()
    ln_gd = nc.dram_tensor("ln_g", [128, NCH], F32, kind="ExternalInput").ap()
    ln_bd = nc.dram_tensor("ln_b", [128, NCH], F32, kind="ExternalInput").ap()
    w1d = nc.dram_tensor("w1", [D, 4096], F32, kind="ExternalInput").ap()
    b1d = nc.dram_tensor("b1", [128, NIO], F32, kind="ExternalInput").ap()
    w2d = nc.dram_tensor("w2", [4096, D], F32, kind="ExternalInput").ap()
    b2d = nc.dram_tensor("b2bc", [128, D], F32, kind="ExternalInput").ap()
    repd = nc.dram_tensor("repmat", [16, 128], F32, kind="ExternalInput").ap()
    hmd = nc.dram_tensor("hmask", [1, HW], F32, kind="ExternalInput").ap()
    out = nc.dram_tensor("out", [T, D], F32, kind="ExternalOutput").ap()
    dbg = None
    if debug:
        dbg = nc.dram_tensor("dbg", [16, T], F32, kind="ExternalOutput").ap()

    live = {}

    def sopen(name, shape, dt, side="left"):
        c = nc.sbuf_tensor(name, shape, dt, side=side)
        h = c.__enter__()
        live[name] = c
        return h

    def sclose(*names):
        for n in names:
            live.pop(n).__exit__(None, None, None)

    def popen(name, shape):
        c = nc.psum_tensor(name, shape, F32)
        h = c.__enter__()
        live[name] = c
        return h

    wcv = sopen("wcv", [128, NCH, 3], F32)
    bdt_s = sopen("bdt_s", [128, NCH], F32)
    g_s = sopen("g_s", [128, NCH], F32)
    bln_s = sopen("bln_s", [128, NCH], F32)
    b1_s = sopen("b1_s", [128, NIO], F32)
    rep_s = sopen("rep_s", [16, 128], F32)
    onesK = sopen("onesK", [16, 1], F32)
    onesB = sopen("onesB", [1, 16], F32)
    meanv = sopen("meanv", [128, 1], BF16)
    tmpc = sopen("tmpc", [128, 1], F32)
    hm_s = sopen("hm_s", [1, HW], F32)
    lnst = sopen("lnst", [128, 12], F32)
    lnmv = sopen("lnmv", [128, 2], F32)
    lnsc = sopen("lnsc", [128, 4], F32)
    cst = sopen("cst", [128, 3], F32)
    lnxT = sopen("lnxT", [128, NFI, 2128], BF16)
    hx = sopen("hx", [128, NCH, 66], F32)
    wdt_s = sopen("wdt_s", [64, DI], BF16)
    waug = sopen("waug", [128, NCH, 82], BF16)
    xs_bf = sopen("xs_bf", [128, NCH, TH], BF16)
    wxp_t = sopen("wxp_t", [128, 2, 96], BF16)
    xt_l = sopen("xt_l", [128, 2, D], F32)
    lnx_bf = sopen("lnx_bf", [128, 2, D], BF16)
    hallo = sopen("hallo", [80, D], F32)
    hallo_b = sopen("hallo_b", [80, D], BF16)
    htr = sopen("htr", [128, 80], BF16)
    wxs_s = sopen("wxs_s", [128, NFI, DI], BF16)
    xz_s = sopen("xz_s", [128, 2, 2116], F32)
    xc_s = sopen("xc_s", [128, TH], F32)

    PA = popen("PA", [128, 2048])
    PB = popen("PB", [128, 1024])
    PC = popen("PC", [128, 1024])
    PAs = [PA[:, s * 512:(s + 1) * 512] for s in range(4)]
    grp_even = PAs
    grp_odd = [PB[:, 0:512], PB[:, 512:1024], PC[:, 0:512], PC[:, 512:1024]]

    SEMS, CNT = {}, {}
    for name in ["s_wxs", "s_wz", "s_wout", "s_w1", "s_wxp0", "s_wxp1",
                 "s_wsm", "s_w2q0", "s_w2q1", "s_w2q2", "s_x0", "s_x1", "s_xb0",
                 "s_xb1", "s_xp0", "s_xp1", "s_tr0", "s_tr1", "s_t20",
                 "s_t21", "s_st0", "s_st1", "s_cp", "s_sx", "s_xh", "s_dbg",
                 "ct", "cv", "cs"]:
        c = nc.semaphore(name)
        SEMS[name] = c.__enter__()
        live["@" + name] = c
        CNT[name] = 0

    def dinc(inst, name):
        inst.then_inc(SEMS[name], 16)
        CNT[name] += 16
        return CNT[name]

    def einc(inst, name):
        inst.then_inc(SEMS[name], 1)
        CNT[name] += 1
        return CNT[name]

    def wt(eng, name, val):
        if val > 0:
            eng.wait_ge(SEMS[name], val)

    M = {}

    def ckpt(name):
        if upto == name:
            raise _StopEmit

    with nc.Block() as blk:
      try:
          @blk.gpsimd
          def _(g):
              for fi in range(NFI):
                  dinc(g.dma_start(out=wxs_s[:, fi, :],
                                   in_=w_in[fi * 128:(fi + 1) * 128, 0:DI]), "s_wxs")
              dinc(g.dma_start(out=wdt_s[:], in_=w_dt[:]), "s_wsm")
              dinc(g.dma_start(out=wcv[:], in_=w_conv[:]), "s_wsm")
              dinc(g.dma_start(out=bdt_s[:], in_=b_dt[:]), "s_wsm")
              dinc(g.dma_start(out=g_s[:], in_=ln_gd[:]), "s_wsm")
              dinc(g.dma_start(out=bln_s[:], in_=ln_bd[:]), "s_wsm")
              dinc(g.dma_start(out=b1_s[:], in_=b1d[:]), "s_wsm")
              dinc(g.dma_start(out=rep_s[:], in_=repd[:]), "s_wsm")
              dinc(g.dma_start(out=hm_s[:], in_=hmd[:]), "s_wsm")
              M["wsm_all"] = CNT["s_wsm"]

          @blk.vector
          def _(v):
              v.memset(meanv[:], 1.0 / 2048.0)
              v.memset(cst[:, 0:1], EPS)
              v.memset(cst[:, 1:2], 1.0)
              v.memset(cst[:, 2:3], -1.0)
              v.memset(onesK[:], 1.0 / 16.0)
              i = v.memset(onesB[:], 1.0)
              i = v.memset(hallo[:], 0.0)
              M["hz"] = einc(i, "cv")

          for k in range(NCH):
              @blk.gpsimd
              def _(g, k=k):
                  if k >= 2:
                      wt(g, "cv", M[("waug", k - 2)])
                  dinc(g.dma_start(out=wxp_t[:, k % 2, :],
                                   in_=w_xproj[k * 128:(k + 1) * 128, :]),
                       "s_wxp%d" % (k % 2))

              @blk.vector
              def _(v, k=k):
                  wt(v, "s_wxp%d" % (k % 2), 16 * (k // 2 + 1))
                  v.drain()
                  v.reduce_sum(tmpc[:], wxp_t[:, k % 2, 64:80], axis=AX.X)
                  v.drain()
                  v.tensor_copy(waug[:, k, 80:81], tmpc[:])
                  v.tensor_copy(waug[:, k, 0:64], wxp_t[:, k % 2, 0:64])
                  v.tensor_copy(waug[:, k, 64:80], wxp_t[:, k % 2, 80:96])
                  i = v.memset(waug[:, k, 81:82], 1.0 / 2048.0)
                  M[("waug", k)] = einc(i, "cv")

          @blk.sync
          def _(s):
              wt(s, "cv", M["hz"])
              dinc(s.dma_start(out=hallo[0:66, :], in_=xhalo[:]), "s_xh")

          def emit_ln_stats(v, xsrc, p, mref):
              v.bn_stats(lnst[0:p, 0:6], xsrc[:, 0:512])
              v.bn_stats(lnst[0:p, 6:12], xsrc[:, 512:1024])
              v.drain()
              v.bn_aggr(lnmv[0:p, :], lnst[0:p, :])
              i = v.drain()
              M[mref] = einc(i, "cv")

          for m in range(NT):
              @blk.sync
              def _(s, m=m):
                  if m >= 2:
                      wt(s, "cv", M[("lnx", m - 2)])
                  dinc(s.dma_start(out=xt_l[:, m % 2, :],
                                   in_=xin[m * 128:(m + 1) * 128, :]),
                       "s_x%d" % (m % 2))

              @blk.vector
              def _(v, m=m):
                  wt(v, "s_x%d" % (m % 2), 16 * (m // 2 + 1))
                  if m >= 2:
                      wt(v, "s_tr%d" % (m % 2), M[("tr1", m - 2)])
                  emit_ln_stats(v, xt_l[:, m % 2, :], 128, ("lnst", m))

              @blk.scalar
              def _(sc, m=m):
                  wt(sc, "cv", M[("lnst", m)])
                  i = sc.activation(lnsc[:, 0:1], lnmv[:, 1:2], AF.Sqrt, bias=cst[:, 0:1])
                  M[("sq", m)] = einc(i, "cs")

              @blk.vector
              def _(v, m=m):
                  wt(v, "cs", M[("sq", m)])
                  v.reciprocal(lnsc[:, 1:2], lnsc[:, 0:1])
                  v.drain()
                  v.tensor_mul(lnsc[:, 2:3], lnmv[:, 0:1], lnsc[:, 1:2])
                  v.drain()
                  i = v.tensor_scalar(lnx_bf[:, m % 2, :], xt_l[:, m % 2, :],
                                      lnsc[:, 1:2], lnsc[:, 2:3],
                                      AL.mult, AL.subtract)
                  M[("lnx", m)] = einc(i, "cv")

              @blk.sync
              def _(s, m=m):
                  wt(s, "cv", M[("lnx", m)])
                  for fc in range(NFI):
                      dinc(s.dma_start_transpose(
                          lnxT[:, fc, m * 128:(m + 1) * 128],
                          lnx_bf[:, m % 2, fc * 128:(fc + 1) * 128]),
                          "s_tr%d" % (m % 2))
                  M[("tr1", m)] = CNT["s_tr%d" % (m % 2)]

          @blk.vector
          def _(v):
              wt(v, "s_xh", 16)
              emit_ln_stats(v, hallo[:], 80, "lnsth")

          @blk.scalar
          def _(sc):
              wt(sc, "cv", M["lnsth"])
              i = sc.activation(lnsc[0:80, 0:1], lnmv[0:80, 1:2], AF.Sqrt, bias=cst[0:80, 0:1])
              M["sqh"] = einc(i, "cs")

          @blk.vector
          def _(v):
              wt(v, "cs", M["sqh"])
              v.reciprocal(lnsc[0:80, 1:2], lnsc[0:80, 0:1])
              v.drain()
              v.tensor_mul(lnsc[0:80, 2:3], lnmv[0:80, 0:1], lnsc[0:80, 1:2])
              v.drain()
              i = v.tensor_scalar(hallo_b[:], hallo[:], lnsc[0:80, 1:2],
                                  lnsc[0:80, 2:3], AL.mult, AL.subtract)
              M["lnxh"] = einc(i, "cv")

          for fc in range(NFI):
              @blk.sync
              def _(s, fc=fc):
                  wt(s, "cv", M["lnxh"] if fc == 0 else M[("htr", fc - 1)])
                  dinc(s.dma_start_transpose(
                      htr[:], hallo_b[0:80, fc * 128:(fc + 1) * 128]), "s_tr0")
                  M[("trh", fc)] = CNT["s_tr0"]

              @blk.vector
              def _(v, fc=fc):
                  wt(v, "s_tr0", M[("trh", fc)])
                  i = v.tensor_copy(lnxT[:, fc, 2048:2114], htr[:, 0:66])
                  M[("htr", fc)] = einc(i, "cv")

          ckpt('ln1')
          for c in range(NCH):
              @blk.tensor
              def _(t, c=c):
                  if c == 0:
                      wt(t, "s_wxs", CNT["s_wxs"])
                      wt(t, "cv", M[("htr", NFI - 1)])
                  ph = PA[:, (c % 2) * 512:(c % 2) * 512 + 66]
                  if c >= 2:
                      wt(t, "cv", M[("hxe", c - 2)])
                  for fi in range(NFI):
                      i = t.matmul(ph, wxs_s[:, fi, c * 128:(c + 1) * 128],
                                   lnxT[:, fi, 2048:2114],
                                   start=(fi == 0), stop=(fi == NFI - 1))
                  M[("hmm", c)] = einc(i, "ct")

              @blk.vector
              def _(v, c=c):
                  wt(v, "ct", M[("hmm", c)])
                  ph = PA[:, (c % 2) * 512:(c % 2) * 512 + 66]
                  i = v.tensor_copy(hx[:, c, :], ph)
                  M[("hxe", c)] = einc(i, "cv")

          ckpt('a0')
          for c in range(NCH):
              grp = grp_even if c % 2 == 0 else grp_odd

              @blk.tensor
              def _(t, c=c, grp=grp):
                  if c == 0:
                      wt(t, "s_tr0", CNT["s_tr0"])
                      wt(t, "s_tr1", CNT["s_tr1"])
                  if c < 2:
                      wt(t, "cv", M[("hxe", NCH - 1)])
                  else:
                      wt(t, "cs", M[("evac", c - 2)])
                  for fi in range(NFI):
                      for s4 in range(NSB):
                          i = t.matmul(grp[s4],
                                       wxs_s[:, fi, c * 128:(c + 1) * 128],
                                       lnxT[:, fi, s4 * 512:(s4 + 1) * 512],
                                       start=(fi == 0), stop=(fi == NFI - 1))
                  M[("xzmm", c)] = einc(i, "ct")

              @blk.vector
              def _(v, c=c):
                  if c == 0:
                      wt(v, "s_wsm", M["wsm_all"])
                  if c >= 2:
                      wt(v, "cs", M[("p1", c - 2)])
                  v.tensor_copy(xz_s[:, c % 2, 0:65], hx[:, c, 0:65])
                  i = v.tensor_copy(xz_s[:, c % 2, 65 + T:66 + T], hx[:, c, 65:66])
                  M[("hcp", c)] = einc(i, "cv")

              @blk.scalar
              def _(sc, c=c, grp=grp):
                  if c == 0:
                      wt(sc, "s_wsm", M["wsm_all"])
                  wt(sc, "ct", M[("xzmm", c)])
                  if c >= 2:
                      wt(sc, "cv", M[("conv", c - 2)])
                  for s4 in range(NSB):
                      i = sc.activation(xz_s[:, c % 2, 65 + s4 * 512:65 + (s4 + 1) * 512],
                                        grp[s4], AF.Copy)
                  M[("evac", c)] = einc(i, "cs")
                  wt(sc, "cv", M[("hcp", c)])
                  sc.drain()
                  i = sc.activation(xc_s[:], xz_s[:, c % 2, 1:1 + TH],
                                    AF.Copy, scale=wcv[:, c, 1:2])
                  M[("p1", c)] = einc(i, "cs")

              @blk.vector
              def _(v, c=c):
                  wt(v, "cs", M[("p1", c)])
                  v.drain()
                  v.scalar_tensor_tensor(xc_s[:], xz_s[:, c % 2, 0:TH],
                                         wcv[:, c, 0:1], xc_s[:],
                                         AL.mult, AL.add)
                  v.drain()
                  i = v.scalar_tensor_tensor(xc_s[:], xz_s[:, c % 2, 2:2 + TH],
                                             wcv[:, c, 2:3], xc_s[:],
                                             AL.mult, AL.add)
                  M[("conv", c)] = einc(i, "cv")

              @blk.scalar
              def _(sc, c=c):
                  wt(sc, "cv", M[("conv", c)])
                  i = sc.activation(xs_bf[:, c, :], xc_s[:], AF.Silu)
                  M[("xs", c)] = einc(i, "cs")

          ckpt('a1')
          sclose("xc_s", "xz_s", "wxs_s", "htr", "hallo_b", "hallo", "lnx_bf",
                 "xt_l", "wxp_t")
          wx2 = sopen("wx2", [64, SB], BF16)
          cpx = sopen("cpx", [128, SB], F32)
          dtw = sopen("dtw", [128, 10, SB], F32)
          dtbf = sopen("dtbf", [128, NCH, SB], BF16)
          rows_a = sopen("rows_a", [1, 3 * TH], F32)
          rowsall = sopen("rowsall", [1, 4096], F32, side="right")
          n16t = sopen("n16t", [16, SB], F32, side="right")
          Cp = sopen("Cp", [16, T], F32, side="right")

          dt_rot = [PAs[1], PAs[2], PAs[3], PB[:, 0:512], PB[:, 512:1024],
                    PC[:, 0:512]]
          for sb, (BW, BOFF) in enumerate(BLOCKS):
              @blk.tensor
              def _(t, sb=sb, BW=BW, BOFF=BOFF):
                  if sb == 0:
                      wt(t, "cs", M[("xs", NCH - 1)])
                      wt(t, "cv", M[("waug", NCH - 1)])
                  else:
                      wt(t, "cs", M[("xpe", sb - 1)])
                  for k in range(NCH):
                      i = t.matmul(PA[0:82, 0:BW], waug[:, k, :],
                                   xs_bf[:, k, BOFF:BOFF + BW],
                                   start=(k == 0), stop=(k == NCH - 1))
                  M[("xp", sb)] = einc(i, "ct")

              ckpt("xpa%d" % sb)

              @blk.scalar
              def _(sc, sb=sb, BW=BW):
                  wt(sc, "ct", M[("xp", sb)])
                  if sb > 0:
                      wt(sc, "s_cp", 16 * (sb - 1))
                      wt(sc, "s_sx", 32 * sb)
                  sc.activation(wx2[0:64, 0:BW], PA[0:64, 0:BW], AF.Copy)
                  i = sc.activation(cpx[64:82, 0:BW], PA[64:82, 0:BW], AF.Copy)
                  M[("xpe", sb)] = einc(i, "cs")

              @blk.gpsimd
              def _(g, sb=sb, BW=BW, BOFF=BOFF):
                  wt(g, "cs", M[("xpe", sb)])
                  if sb > 0:
                      dinc(g.dma_start(out=Cp[:, BOFF - 64:BOFF - 64 + BW],
                                       in_=cpx[64:80, 0:BW]), "s_cp")
                  dinc(g.dma_start(
                      out=rows_a[0:1, R_SUMBP + BOFF:R_SUMBP + BOFF + BW],
                      in_=cpx[80:81, 0:BW]), "s_sx")
                  dinc(g.dma_start(
                      out=rows_a[0:1, R_XM + BOFF:R_XM + BOFF + BW],
                      in_=cpx[81:82, 0:BW]), "s_sx")

              ckpt("mv%d" % sb)

              for mc in range(NCH):
                  ckpt("mc%d_%d" % (sb, mc))

                  @blk.tensor
                  def _(t, sb=sb, mc=mc, BW=BW):
                      grp = dt_rot[mc % 6]
                      last = mc - 6 if mc >= 6 else (mc + 12 if mc + 12 < NCH else mc + 6)
                      lk = (sb, mc - 6) if mc >= 6 else \
                           ((sb - 1, last) if sb > 0 else None)
                      if lk is not None:
                          wt(t, "cs", M[("dty",) + lk])
                      if mc == 0:
                          wt(t, "cs", M[("xpe", sb)])
                      i = t.matmul(grp[:, 0:BW], wdt_s[:, mc * 128:(mc + 1) * 128],
                                   wx2[0:64, 0:BW], start=True, stop=True)
                      M[("dt", sb, mc)] = einc(i, "ct")

                  @blk.scalar
                  def _(sc, sb=sb, mc=mc, BW=BW):
                      b = (mc % 2) * 5
                      grp = dt_rot[mc % 6]
                      wt(sc, "ct", M[("dt", sb, mc)])
                      if mc >= 2:
                          wt(sc, "cv", M[("dtrelu", sb, mc - 2)])
                      elif sb > 0:
                          wt(sc, "cv", M[("dtrelu", sb - 1, NCH - 2 + mc)])
                      i = sc.activation(dtw[:, b, 0:BW], grp[:, 0:BW], AF.Identity,
                                        bias=bdt_s[:, mc:mc + 1])
                      M[("dty", sb, mc)] = einc(i, "cs")
                      sc.drain()
                      sc.activation(dtw[:, b + 1, 0:BW], dtw[:, b, 0:BW], AF.Abs)
                      sc.drain()
                      i = sc.activation(dtw[:, b + 2, 0:BW], dtw[:, b + 1, 0:BW],
                                        AF.Exp, scale=cst[:, 2:3])
                      M[("dtexp", sb, mc)] = einc(i, "cs")

                  @blk.vector
                  def _(v, sb=sb, mc=mc, BW=BW):
                      b = (mc % 2) * 5
                      wt(v, "cs", M[("dty", sb, mc)])
                      if sb > 0 and mc == 0:
                          wt(v, "ct", M[("dtm", sb - 1)])
                      i = v.tensor_single_scalar(dtw[:, b + 4, 0:BW], dtw[:, b, 0:BW],
                                                 0.0, AL.max)
                      M[("dtrelu", sb, mc)] = einc(i, "cv")
                      wt(v, "cs", M[("dtexp", sb, mc)])
                      v.drain()
                      i = v.tensor_single_scalar(dtw[:, b + 2, 0:BW], dtw[:, b + 2, 0:BW],
                                                 1.0, AL.add)
                      M[("dtep1", sb, mc)] = einc(i, "cv")

                  @blk.scalar
                  def _(sc, sb=sb, mc=mc, BW=BW):
                      b = (mc % 2) * 5
                      wt(sc, "cv", M[("dtep1", sb, mc)])
                      if mc >= 2:
                          wt(sc, "cv", M[("dtbf", sb, mc - 2)])
                      elif sb > 0:
                          wt(sc, "cv", M[("dtbf", sb - 1, NCH - 2 + mc)])
                      i = sc.activation(dtw[:, b + 3, 0:BW], dtw[:, b + 2, 0:BW], AF.Ln)
                      M[("dtln", sb, mc)] = einc(i, "cs")

                  @blk.vector
                  def _(v, sb=sb, mc=mc, BW=BW):
                      b = (mc % 2) * 5
                      wt(v, "cs", M[("dtln", sb, mc)])
                      v.drain()
                      i = v.tensor_add(dtbf[:, mc, 0:BW], dtw[:, b + 3, 0:BW],
                                       dtw[:, b + 4, 0:BW])
                      M[("dtbf", sb, mc)] = einc(i, "cv")

              ckpt("dt%d" % sb)

              @blk.tensor
              def _(t, sb=sb, BW=BW):
                  wt(t, "cv", M[("dtbf", sb, NCH - 1)])
                  if sb > 0:
                      wt(t, "cs", M[("dtmrow", sb - 1)])
                  for mc in range(NCH):
                      i = t.matmul(PC[0:1, 512:512 + BW], meanv[:], dtbf[:, mc, 0:BW],
                                   start=(mc == 0), stop=(mc == NCH - 1))
                  M[("dtm", sb)] = einc(i, "ct")

              ckpt("dm%d" % sb)

              @blk.scalar
              def _(sc, sb=sb, BW=BW, BOFF=BOFF):
                  wt(sc, "ct", M[("dtm", sb)])
                  i = sc.activation(rows_a[0:1, R_DTM + BOFF:R_DTM + BOFF + BW],
                                    PC[0:1, 512:512 + BW], AF.Copy)
                  M[("dtmrow", sb)] = einc(i, "cs")

          @blk.scalar
          def _(sc):
              sc.drain()
              i = sc.activation(rows_a[0:1, R_DTM:R_DTM + TH],
                                rows_a[0:1, R_DTM:R_DTM + TH], AF.Exp, scale=cst[0:1, 2:3])
              M["decay"] = einc(i, "cs")

          @blk.vector
          def _(v):
              wt(v, "s_sx", CNT["s_sx"])
              wt(v, "s_wsm", M["wsm_all"])
              v.tensor_mul(rows_a[0:1, R_SUMBP:R_SUMBP + TH],
                           rows_a[0:1, R_SUMBP:R_SUMBP + TH],
                           rows_a[0:1, R_XM:R_XM + TH])
              v.drain()
              v.tensor_mul(rows_a[0:1, R_SUMBP:R_SUMBP + HW],
                           rows_a[0:1, R_SUMBP:R_SUMBP + HW], hm_s[:])
              wt(v, "cs", M["decay"])
              v.drain()
              v.tensor_tensor_scan(rows_a[0:1, R_XM:R_XM + TH],
                                   rows_a[0:1, R_DTM:R_DTM + TH],
                                   rows_a[0:1, R_SUMBP:R_SUMBP + TH],
                                   0.0, AL.mult, AL.add)
              v.drain()
              i = v.tensor_copy(rowsall[0:1, 2048:4096],
                                rows_a[0:1, R_XM + HW:R_XM + HW + T])
              M["scan"] = einc(i, "cv")

          if debug:
              @blk.gpsimd
              def _(g):
                  wt(g, "cv", M["scan"])
                  wt(g, "s_cp", CNT["s_cp"])
                  dinc(g.dma_start(out=dbg[0:1, :],
                                   in_=rows_a[0:1, R_DTM + HW:R_DTM + HW + T]), "s_dbg")
                  dinc(g.dma_start(out=dbg[1:2, :],
                                   in_=rows_a[0:1, R_SUMBP + HW:R_SUMBP + HW + T]), "s_dbg")
                  dinc(g.dma_start(out=dbg[2:3, :], in_=rowsall[0:1, 2048:4096]), "s_dbg")
                  dinc(g.dma_start(out=dbg[3:4, :], in_=Cp[0:1, :]), "s_dbg")
                  M["dbg1"] = CNT["s_dbg"]

          ckpt('scan')
          sclose("rows_a", "dtbf", "dtw", "cpx", "wx2", "xs_bf", "waug", "wdt_s", "hx")
          wz_s = sopen("wz_s", [128, NFI, DI], BF16)
          rowsb = sopen("rowsb", [1, 4096], F32)
          z_a = sopen("z_a", [128, NCH, 1024], BF16, side="right")
          z_b = sopen("z_b", [128, NCH, 1024], BF16, side="right")
          wout_s = sopen("wout_s", [128, NCH, D], BF16, side="right")

          @blk.gpsimd
          def _(g):
              wt(g, "cv", M["scan"])
              wt(g, "s_cp", CNT["s_cp"])
              if debug:
                  wt(g, "s_dbg", M["dbg1"])
              for fi in range(NFI):
                  if fi >= 1:
                      wt(g, "s_wz", M[("wzc", fi - 1)])
                  M[("wzc", fi)] = dinc(
                      g.dma_start(out=wz_s[:, fi, :],
                                  in_=w_in[fi * 128:(fi + 1) * 128, DI:2 * DI]), "s_wz")
              for k in range(NCH):
                  if k >= 1:
                      wt(g, "s_wout", M[("woc", k - 1)])
                  M[("woc", k)] = dinc(
                      g.dma_start(out=wout_s[:, k, :],
                                  in_=w_outp[k * 128:(k + 1) * 128, :]), "s_wout")

          mcp2g = [PB[0:1, 0:512], PB[0:1, 512:1024], PC[0:1, 0:512], PC[0:1, 512:1024]]

          for k in range(NSB):
              @blk.vector
              def _(v, k=k):
                  if k == 0:
                      wt(v, "cv", M["scan"])
                      wt(v, "s_cp", CNT["s_cp"])
                  else:
                      wt(v, "ct", M[("chmm", k - 1)])
                  v.drain()
                  i = v.tensor_mul(n16t[:], Cp[:, k * 512:(k + 1) * 512],
                                   Cp[:, k * 512:(k + 1) * 512])
                  M[("cp2", k)] = einc(i, "cv")

              @blk.tensor
              def _(t, k=k):
                  if k == 0:
                      wt(t, "cs", M[("dtmrow", NSB - 1)])
                      wt(t, "cv", M["scan"])
                  wt(t, "cv", M[("cp2", k)])
                  t.matmul(PA[0:1, k * 512:(k + 1) * 512], onesK[:],
                           Cp[:, k * 512:(k + 1) * 512], start=True, stop=True)
                  i = t.matmul(mcp2g[k], onesK[:], n16t[:], start=True, stop=True)
                  M[("chmm", k)] = einc(i, "ct")

          @blk.scalar
          def _(sc):
              wt(sc, "ct", M[("chmm", NSB - 1)])
              for k in range(NSB):
                  sc.activation(rowsall[0:1, k * 512:(k + 1) * 512],
                                PA[0:1, k * 512:(k + 1) * 512], AF.Copy)
                  i = sc.activation(rowsb[0:1, k * 512:(k + 1) * 512],
                                    mcp2g[k], AF.Copy)
              M["chev"] = einc(i, "cs")

          @blk.vector
          def _(v):
              wt(v, "cs", M["chev"])
              v.tensor_mul(rowsb[0:1, 2048:4096], rowsall[0:1, 0:2048],
                           rowsall[0:1, 0:2048])
              v.drain()
              v.tensor_sub(rowsb[0:1, 0:2048], rowsb[0:1, 0:2048],
                           rowsb[0:1, 2048:4096])
              v.drain()
              v.tensor_mul(rowsb[0:1, 2048:4096], rowsall[0:1, 2048:4096], rowsall[0:1, 2048:4096])
              v.drain()
              i = v.tensor_mul(rowsb[0:1, 0:2048], rowsb[0:1, 2048:4096],
                               rowsb[0:1, 0:2048])
              M["chq"] = einc(i, "cv")

          @blk.scalar
          def _(sc):
              wt(sc, "cv", M["chq"])
              i = sc.activation(rowsb[0:1, 2048:4096], rowsb[0:1, 0:2048],
                                AF.Sqrt, bias=cst[0:1, 0:1])
              M["chsq"] = einc(i, "cs")

          @blk.vector
          def _(v):
              wt(v, "cs", M["chsq"])
              if debug:
                  wt(v, "s_dbg", M["dbg1"])
              v.reciprocal(rowsb[0:1, 0:2048], rowsb[0:1, 2048:4096])
              v.drain()
              i = v.tensor_mul(rowsall[0:1, 2048:4096], rowsall[0:1, 2048:4096],
                               rowsb[0:1, 0:2048])
              M["chsr"] = einc(i, "cv")

          for c in range(NCH):
              grp = grp_even if c % 2 == 0 else grp_odd

              @blk.tensor
              def _(t, c=c, grp=grp):
                  if c == 0:
                      wt(t, "cs", M["chev"])
                  if c >= 2:
                      wt(t, "cs", M[("zevac", c - 2)])
                  for fi in range(NFI):
                      if c == 0:
                          wt(t, "s_wz", M[("wzc", fi)])
                      for s4 in range(NSB):
                          i = t.matmul(grp[s4], wz_s[:, fi, c * 128:(c + 1) * 128],
                                       lnxT[:, fi, s4 * 512:(s4 + 1) * 512],
                                       start=(fi == 0), stop=(fi == NFI - 1))
                  M[("zmm", c)] = einc(i, "ct")

              @blk.scalar
              def _(sc, c=c, grp=grp):
                  wt(sc, "ct", M[("zmm", c)])
                  for s4 in range(NSB):
                      ztgt = z_a if s4 < 2 else z_b
                      i = sc.activation(ztgt[:, c, (s4 % 2) * 512:(s4 % 2 + 1) * 512],
                                        grp[s4], AF.Silu)
                  M[("zevac", c)] = einc(i, "cs")

          ckpt('a2')
          sclose("rowsb", "wz_s", "lnxT")
          lnx2T = sopen("lnx2T", [128, NFI, T], BF16)
          gt = sopen("gt", [128, 2, SB], F32)
          y_g = sopen("y_g", [128, NCH, SB], BF16)
          xt_b = sopen("xt_b", [128, 2, D], F32)
          x1t = sopen("x1t", [128, 2, D], F32)
          lnx2_bf = sopen("lnx2_bf", [128, 2, D], BF16)

          for sb in range(NSB):
              zt = z_a if sb < 2 else z_b

              @blk.tensor
              def _(t, sb=sb):
                  wt(t, "cv", M["chsr"])
                  if sb == 0:
                      wt(t, "s_wsm", M["wsm_all"])
                      wt(t, "cs", M[("zevac", NCH - 1)])
                  if sb > 0:
                      wt(t, "cv", M[("n16", sb - 1)])
                  t.matmul(PA[0:16, 1024:1536], onesB[:],
                           rowsall[0:1, sb * 512:(sb + 1) * 512], start=True, stop=True)
                  i = t.matmul(PA[0:16, 1536:2048], onesB[:],
                               rowsall[0:1, 2048 + sb * 512:2048 + (sb + 1) * 512],
                               start=True, stop=True)
                  M[("bc", sb)] = einc(i, "ct")

              @blk.vector
              def _(v, sb=sb):
                  wt(v, "ct", M[("bc", sb)])
                  v.tensor_sub(n16t[:], Cp[:, sb * 512:(sb + 1) * 512],
                               PA[0:16, 1024:1536])
                  v.drain()
                  i = v.tensor_mul(n16t[:], n16t[:], PA[0:16, 1536:2048])
                  M[("n16", sb)] = einc(i, "cv")

              @blk.tensor
              def _(t, sb=sb):
                  wt(t, "cv", M[("n16", sb)])
                  if sb > 0:
                      wt(t, "cv", M[("gate", sb - 1, NCH - 1)])
                  i = t.matmul(PB[:, 0:512], rep_s[:], n16t[:], start=True, stop=True)
                  M[("tiled", sb)] = einc(i, "ct")

              for c in range(NCH):
                  @blk.scalar
                  def _(sc, sb=sb, c=c):
                      if c == 0:
                          wt(sc, "ct", M[("tiled", sb)])
                      if c >= 2:
                          wt(sc, "cv", M[("gate", sb, c - 2)])
                      elif sb > 0:
                          wt(sc, "cv", M[("gate", sb - 1, NCH - 2 + c)])
                      i = sc.activation(gt[:, c % 2, :], PB[:, 0:512], AF.Identity,
                                        bias=bln_s[:, c:c + 1],
                                        scale=g_s[:, c:c + 1])
                      M[("gp1", sb, c)] = einc(i, "cs")

                  @blk.vector
                  def _(v, sb=sb, c=c, zt=zt):
                      if c == 0 and sb > 0:
                          wt(v, "ct", M[("wo", sb - 1, 3)])
                      wt(v, "cs", M[("gp1", sb, c)])
                      wt(v, "cs", M[("zevac", c)])
                      v.drain()
                      i = v.tensor_mul(y_g[:, c, :], gt[:, c % 2, :],
                                       zt[:, c, (sb % 2) * 512:(sb % 2 + 1) * 512])
                      M[("gate", sb, c)] = einc(i, "cv")

              for tb in range(4):
                  m = sb * 4 + tb
                  ptok = PC if tb % 2 == 0 else PB

                  @blk.tensor
                  def _(t, sb=sb, tb=tb, ptok=ptok):
                      if tb == 1:
                          wt(t, "cs", M[("gp1", sb, NCH - 1)])
                      if tb >= 2:
                          wt(t, "cv", M[("x1", sb, tb - 2)])
                      for c in range(NCH):
                          wt(t, "cv", M[("gate", sb, c)])
                          if sb == 0 and tb == 0:
                              wt(t, "s_wout", M[("woc", c)])
                          t.matmul(ptok[:, 0:512],
                                   y_g[:, c, tb * 128:(tb + 1) * 128],
                                   wout_s[:, c, 0:512],
                                   start=(c == 0), stop=(c == NCH - 1))
                          i = t.matmul(ptok[:, 512:1024],
                                       y_g[:, c, tb * 128:(tb + 1) * 128],
                                       wout_s[:, c, 512:1024],
                                       start=(c == 0), stop=(c == NCH - 1))
                      M[("wo", sb, tb)] = einc(i, "ct")

                  @blk.sync
                  def _(s, m=m):
                      if m == 0:
                          wt(s, "cs", M[("zevac", NCH - 1)])
                      if m >= 2:
                          wt(s, "cv", M[("lnx2", m - 2)])
                      dinc(s.dma_start(out=xt_b[:, m % 2, :],
                                       in_=xin[m * 128:(m + 1) * 128, :]),
                           "s_xb%d" % (m % 2))

                  @blk.vector
                  def _(v, sb=sb, tb=tb, m=m, ptok=ptok):
                      wt(v, "ct", M[("wo", sb, tb)])
                      wt(v, "s_xb%d" % (m % 2), 16 * (m // 2 + 1))
                      if m >= 2:
                          wt(v, "s_st%d" % (m % 2), M[("spill", m - 2)])
                          wt(v, "s_t2%d" % (m % 2), M[("tr2", m - 2)])
                      i = v.tensor_add(x1t[:, m % 2, :], ptok[:], xt_b[:, m % 2, :])
                      M[("x1", sb, tb)] = einc(i, "cv")
                      v.drain()
                      emit_ln_stats(v, x1t[:, m % 2, :], 128, ("lnst2", m))

                  @blk.scalar
                  def _(sc, m=m):
                      wt(sc, "cv", M[("lnst2", m)])
                      i = sc.activation(lnsc[:, 0:1], lnmv[:, 1:2], AF.Sqrt, bias=cst[:, 0:1])
                      M[("sq2", m)] = einc(i, "cs")

                  @blk.vector
                  def _(v, m=m):
                      wt(v, "cs", M[("sq2", m)])
                      v.reciprocal(lnsc[:, 1:2], lnsc[:, 0:1])
                      v.drain()
                      v.tensor_mul(lnsc[:, 2:3], lnmv[:, 0:1], lnsc[:, 1:2])
                      v.drain()
                      i = v.tensor_scalar(lnx2_bf[:, m % 2, :], x1t[:, m % 2, :],
                                          lnsc[:, 1:2], lnsc[:, 2:3],
                                          AL.mult, AL.subtract)
                      M[("lnx2", m)] = einc(i, "cv")

                  @blk.sync
                  def _(s, sb=sb, tb=tb, m=m):
                      wt(s, "cv", M[("x1", sb, tb)])
                      M[("spill", m)] = dinc(
                          s.dma_start(out=out[m * 128:(m + 1) * 128, :],
                                      in_=x1t[:, m % 2, :]), "s_st%d" % (m % 2))
                      wt(s, "cv", M[("lnx2", m)])
                      for fc in range(NFI):
                          dinc(s.dma_start_transpose(
                              lnx2T[:, fc, m * 128:(m + 1) * 128],
                              lnx2_bf[:, m % 2, fc * 128:(fc + 1) * 128]),
                              "s_t2%d" % (m % 2))
                      M[("tr2", m)] = CNT["s_t2%d" % (m % 2)]

          ckpt('b1')
          sclose("lnx2_bf", "x1t", "xt_b", "y_g", "gt")
          sclose("wout_s", "z_b", "z_a", "Cp", "n16t", "rowsall")
          w1_s = sopen("w1_s", [128, NFI, 4096], BF16)
          h_sb = sopen("h_sb", [128, NIO, SB], BF16)
          w2buf = sopen("w2buf", [128, 3, 8, D], BF16)
          xpt = sopen("xpt", [128, 2, D], F32)
          ot = sopen("ot", [128, 2, D], F32)
          b2_s = sopen("b2_s", [128, D], F32)
          w2r = w2d.rearrange("(j p) d -> p j d", p=128)

          @blk.gpsimd
          def _(g):
              wt(g, "cv", M[("lnx2", NT - 1)])
              wt(g, "s_st0", CNT["s_st0"])
              wt(g, "s_st1", CNT["s_st1"])
              wt(g, "s_t20", CNT["s_t20"])
              wt(g, "s_t21", CNT["s_t21"])
              M["b2"] = dinc(g.dma_start(out=b2_s[:], in_=b2d[:]), "s_wsm")
              w1r = w1d.rearrange("(k p) d -> p k d", p=128)
              for pn in range(8):
                  if pn >= 1:
                      wt(g, "s_w1", M[("w1p", pn - 1)])
                  M[("w1p", pn)] = dinc(
                      g.dma_start(out=w1_s[:, :, pn * 512:(pn + 1) * 512],
                                  in_=w1r[:, :, pn * 512:(pn + 1) * 512]), "s_w1")

          for sb in range(NSB):
              for io in range(NIO):
                  @blk.tensor
                  def _(t, sb=sb, io=io):
                      if sb == 0 and io == 0:
                          wt(t, "s_t20", CNT["s_t20"])
                          wt(t, "s_t21", CNT["s_t21"])
                          wt(t, "cv", M[("x1", NSB - 1, 3)])
                      if sb == 0 and io % 4 == 0:
                          wt(t, "s_w1", M[("w1p", io // 4)])
                      grp = PAs[io % 4]
                      fkey = ("hfree", sb, io - 4) if io >= 4 else \
                             (("hfree", sb - 1, io + NIO - 4) if sb > 0 else None)
                      if fkey is not None:
                          wt(t, "cs", M[fkey])
                      for fi in range(NFI):
                          i = t.matmul(grp, w1_s[:, fi, io * 128:(io + 1) * 128],
                                       lnx2T[:, fi, sb * 512:(sb + 1) * 512],
                                       start=(fi == 0), stop=(fi == NFI - 1))
                      M[("w1", sb, io)] = einc(i, "ct")

                  @blk.scalar
                  def _(sc, sb=sb, io=io):
                      wt(sc, "ct", M[("w1", sb, io)])
                      i = sc.activation(h_sb[:, io, :], PAs[io % 4], AF.Relu,
                                        bias=b1_s[:, io:io + 1])
                      M[("hfree", sb, io)] = einc(i, "cs")

              for p2 in range(2):
                  tbs = (0, 1) if p2 == 0 else (2, 3)

                  for qh in range(2):
                      @blk.gpsimd
                      def _(g, sb=sb, p2=p2, qh=qh):
                          for q in (2 * qh, 2 * qh + 1):
                              qi = sb * 8 + p2 * 4 + q
                              if qi >= 3:
                                  wt(g, "ct", M[("w2used", qi - 3)])
                              M[("w2ld", qi)] = dinc(
                                  g.dma_start(out=w2buf[:, qi % 3, :, :],
                                              in_=w2r[:, q * 8:(q + 1) * 8, :]),
                                  "s_w2q%d" % (qi % 3))

                      @blk.tensor
                      def _(t, sb=sb, p2=p2, qh=qh, tbs=tbs):
                          if qh == 0 and not (sb == 0 and p2 == 0):
                              pm = (sb * 4 + 1) if p2 == 1 else ((sb - 1) * 4 + 3)
                              wt(t, "cv", M[("outv", pm)])
                          for q in (2 * qh, 2 * qh + 1):
                              qi = sb * 8 + p2 * 4 + q
                              wt(t, "s_w2q%d" % (qi % 3), M[("w2ld", qi)])
                              for j in range(8):
                                  io = q * 8 + j
                                  wt(t, "cs", M[("hfree", sb, io)])
                                  for tbi, tb in enumerate(tbs):
                                      pt = PC if tbi == 0 else PB
                                      t.matmul(pt[:, 0:512],
                                               h_sb[:, io, tb * 128:(tb + 1) * 128],
                                               w2buf[:, qi % 3, j, 0:512],
                                               start=(io == 0), stop=(io == NIO - 1))
                                      i = t.matmul(pt[:, 512:1024],
                                                   h_sb[:, io, tb * 128:(tb + 1) * 128],
                                                   w2buf[:, qi % 3, j, 512:1024],
                                                   start=(io == 0), stop=(io == NIO - 1))
                              M[("w2used", qi)] = einc(i, "ct")
                          if qh == 1:
                              M[("w2", sb, p2)] = CNT["ct"]

                  for tbi, tb in enumerate(tbs):
                      m = sb * 4 + tb
                      pt = PC if tbi == 0 else PB

                      @blk.sync
                      def _(s, m=m):
                          if m == 0:
                              wt(s, "s_st0", CNT["s_st0"])
                              wt(s, "s_st1", CNT["s_st1"])
                          if m >= 2:
                              wt(s, "cv", M[("outv", m - 2)])
                          dinc(s.dma_start(out=xpt[:, m % 2, :],
                                           in_=out[m * 128:(m + 1) * 128, :]),
                               "s_xp%d" % (m % 2))

                      @blk.vector
                      def _(v, sb=sb, p2=p2, m=m, pt=pt):
                          wt(v, "ct", M[("w2", sb, p2)])
                          wt(v, "s_xp%d" % (m % 2), 16 * (m // 2 + 1))
                          if m == 0:
                              wt(v, "s_wsm", M["b2"])
                          if m >= 2:
                              wt(v, "s_st%d" % (m % 2), M[("outd", m - 2)])
                          v.tensor_add(ot[:, m % 2, :], pt[:], b2_s[:])
                          v.drain()
                          i = v.tensor_add(ot[:, m % 2, :], ot[:, m % 2, :],
                                           xpt[:, m % 2, :])
                          M[("outv", m)] = einc(i, "cv")

                      @blk.sync
                      def _(s, m=m):
                          wt(s, "cv", M[("outv", m)])
                          M[("outd", m)] = dinc(
                              s.dma_start(out=out[m * 128:(m + 1) * 128, :],
                                          in_=ot[:, m % 2, :]), "s_st%d" % (m % 2))

          @blk.sync
          def _(s):
              s.wait_ge(SEMS["s_st0"], CNT["s_st0"])
              s.wait_ge(SEMS["s_st1"], CNT["s_st1"])

      except _StopEmit:
        pass
    for n in reversed(list(live)):
        live.pop(n).__exit__(None, None, None)
    return nc


def _prep_inputs(inputs):
    X = np.ascontiguousarray(
        np.asarray(inputs["L0_attn_x"], np.float32).reshape(2 * 8192, 1024))
    W_in = np.ascontiguousarray(np.asarray(inputs["L0_attn_W_in"], np.float32))
    W_conv = np.asarray(inputs["L0_attn_W_conv"], np.float32)[:, 0, :]
    wcv = np.ascontiguousarray(
        W_conv.reshape(NCH, 128, 3).transpose(1, 0, 2).reshape(128, NCH * 3))
    W_xproj = np.ascontiguousarray(np.asarray(inputs["L0_attn_W_xproj"], np.float32))
    W_dt = np.ascontiguousarray(np.asarray(inputs["L0_attn_W_dt"], np.float32))
    b_dt = np.ascontiguousarray(
        np.asarray(inputs["L0_attn_b_dt"], np.float32).reshape(NCH, 128).T)
    W_out = np.ascontiguousarray(np.asarray(inputs["L0_attn_W_out"], np.float32))
    ln_g = np.ascontiguousarray(
        np.asarray(inputs["L0_attn_ln_g"], np.float32).reshape(NCH, 128).T)
    ln_b = np.ascontiguousarray(
        np.asarray(inputs["L0_attn_ln_b"], np.float32).reshape(NCH, 128).T)
    W1 = np.ascontiguousarray(np.asarray(inputs["L0_ffn_W1"], np.float32))
    b1 = np.ascontiguousarray(
        np.asarray(inputs["L0_ffn_b1"], np.float32).reshape(NIO, 128).T)
    W2 = np.ascontiguousarray(np.asarray(inputs["L0_ffn_W2"], np.float32))
    b2 = np.ascontiguousarray(np.broadcast_to(
        np.asarray(inputs["L0_ffn_b2"], np.float32)[None, :], (128, 1024)).copy())
    rep = np.zeros((16, 128), np.float32)
    for s in range(16):
        rep[s, s::16] = 1.0

    per_batch = 8192 // T
    in_maps = []
    for c in range(8):
        t0 = c * T
        halo = np.zeros((66, 1024), np.float32)
        if (c % per_batch) != 0:
            halo[0:65] = X[t0 - 65:t0]
        if (c % per_batch) != per_batch - 1:
            halo[65] = X[t0 + T]
        hmask = np.full((1, 64), 0.0 if (c % per_batch) == 0 else 1.0, np.float32)
        in_maps.append({
            "xin": np.ascontiguousarray(X[t0:t0 + T]),
            "xhalo": halo,
            "hmask": hmask,
            "w_in": W_in, "w_conv": wcv, "w_xproj": W_xproj, "w_dt": W_dt,
            "b_dt": b_dt, "w_outp": W_out, "ln_g": ln_g, "ln_b": ln_b,
            "w1": W1, "b1": b1, "w2": W2, "b2bc": b2, "repmat": rep,
        })
    return in_maps


def kernel(**inputs):
    in_maps = _prep_inputs(inputs)
    nc = build_nc(debug=False)
    res = run_bass_kernel_spmd(nc, in_maps, core_ids=list(range(8)))
    outs = [np.asarray(r["out"]) for r in res.results]
    return np.concatenate(outs, axis=0).reshape(2, 8192, 1024).astype(np.float32)


# version 11
# speedup vs baseline: 1.0938x; 1.0938x over previous
import numpy as np

import concourse.bass as bass
import concourse.mybir as mybir
from concourse.bass_utils import run_bass_kernel_spmd

F32 = mybir.dt.float32
BF16 = mybir.dt.bfloat16
AF = mybir.ActivationFunctionType
AL = mybir.AluOpType
AX = mybir.AxisListType

T = 2048
SB = 512
NSB = T // SB
NT = T // 128
NCH = 16
NFI = 8
NIO = 32
D = 1024
DI = 2048
EPS = 1e-5

HW = 64
TH = T + HW
R_DTM = 0
R_SUMBP = TH
R_XM = 2 * TH
BLOCKS = [(64, 0), (512, 64), (512, 576), (512, 1088), (512, 1600)]
RB_MC = 0
RB_X = 512
RB_Y = 1024
RB_Z = 1536


class _StopEmit(Exception):
    pass


def build_nc(debug=False, upto="end"):
    nc = bass.Bass("TRN2", target_bir_lowering=False)

    xin = nc.dram_tensor("xin", [T, D], F32, kind="ExternalInput").ap()
    xhalo = nc.dram_tensor("xhalo", [66, D], F32, kind="ExternalInput").ap()
    w_in = nc.dram_tensor("w_in", [D, 2 * DI], F32, kind="ExternalInput").ap()
    w_conv = nc.dram_tensor("w_conv", [128, NCH * 3], F32, kind="ExternalInput").ap()
    w_xproj = nc.dram_tensor("w_xproj", [DI, 96], F32, kind="ExternalInput").ap()
    w_dt = nc.dram_tensor("w_dt", [64, DI], F32, kind="ExternalInput").ap()
    b_dt = nc.dram_tensor("b_dt", [128, NCH], F32, kind="ExternalInput").ap()
    w_outp = nc.dram_tensor("w_outp", [DI, D], F32, kind="ExternalInput").ap()
    ln_gd = nc.dram_tensor("ln_g", [128, NCH], F32, kind="ExternalInput").ap()
    ln_bd = nc.dram_tensor("ln_b", [128, NCH], F32, kind="ExternalInput").ap()
    w1d = nc.dram_tensor("w1", [D, 4096], F32, kind="ExternalInput").ap()
    b1d = nc.dram_tensor("b1", [128, NIO], F32, kind="ExternalInput").ap()
    w2d = nc.dram_tensor("w2", [4096, D], F32, kind="ExternalInput").ap()
    b2d = nc.dram_tensor("b2bc", [128, D], F32, kind="ExternalInput").ap()
    repd = nc.dram_tensor("repmat", [16, 128], F32, kind="ExternalInput").ap()
    hmd = nc.dram_tensor("hmask", [1, HW], F32, kind="ExternalInput").ap()
    out = nc.dram_tensor("out", [T, D], F32, kind="ExternalOutput").ap()
    dbg = None
    if debug:
        dbg = nc.dram_tensor("dbg", [16, T], F32, kind="ExternalOutput").ap()

    live = {}

    def sopen(name, shape, dt, side="left"):
        c = nc.sbuf_tensor(name, shape, dt, side=side)
        h = c.__enter__()
        live[name] = c
        return h

    def sclose(*names):
        for n in names:
            live.pop(n).__exit__(None, None, None)

    def popen(name, shape):
        c = nc.psum_tensor(name, shape, F32)
        h = c.__enter__()
        live[name] = c
        return h

    wcv = sopen("wcv", [128, NCH, 3], F32)
    bdt_s = sopen("bdt_s", [128, NCH], F32)
    g_s = sopen("g_s", [128, NCH], F32)
    bln_s = sopen("bln_s", [128, NCH], F32)
    b1_s = sopen("b1_s", [128, NIO], F32)
    rep_s = sopen("rep_s", [16, 128], F32)
    onesK = sopen("onesK", [16, 1], F32)
    onesB = sopen("onesB", [1, 16], F32)
    meanv = sopen("meanv", [128, 1], BF16)
    tmpc = sopen("tmpc", [128, 1], F32)
    hm_s = sopen("hm_s", [1, HW], F32)
    lnst = sopen("lnst", [128, 12], F32)
    lnmv = sopen("lnmv", [128, 2], F32)
    lnsc = sopen("lnsc", [128, 4], F32)
    cst = sopen("cst", [128, 4], F32)
    lnxT = sopen("lnxT", [128, NFI, 2128], BF16)
    hx = sopen("hx", [128, NCH, 66], F32)
    wdt_s = sopen("wdt_s", [64, DI], BF16)
    waug = sopen("waug", [128, NCH, 82], BF16)
    xs_ch = sopen("xs_ch", [128, 2, TH], BF16)
    xs_bh = sopen("xs_bh", [128, NCH, HW], BF16)
    wxp_t = sopen("wxp_t", [128, 2, 96], BF16)
    xt_l = sopen("xt_l", [128, 2, D], F32)
    lnx_bf = sopen("lnx_bf", [128, 2, D], BF16)
    hallo = sopen("hallo", [80, D], F32)
    hallo_b = sopen("hallo_b", [80, D], BF16)
    htr = sopen("htr", [128, 80], BF16)
    wxs_s = sopen("wxs_s", [128, NFI, DI], BF16)
    xz_s = sopen("xz_s", [128, 2, 2116], F32)
    xc_s = sopen("xc_s", [128, TH], F32)

    PA = popen("PA", [128, 2048])
    PB = popen("PB", [128, 1024])
    PC = popen("PC", [128, 1024])
    PAs = [PA[:, s * 512:(s + 1) * 512] for s in range(4)]
    grp_even = PAs
    grp_odd = [PB[:, 0:512], PB[:, 512:1024], PC[:, 0:512], PC[:, 512:1024]]

    SEMS, CNT = {}, {}
    for name in ["s_wxs", "s_wz", "s_wout", "s_w1", "s_wxp0", "s_wxp1",
                 "s_wsm", "s_w2q0", "s_w2q1", "s_w2q2", "s_x0", "s_x1", "s_xb0",
                 "s_xb1", "s_xp0", "s_xp1", "s_tr0", "s_tr1", "s_t20",
                 "s_t21", "s_st0", "s_st1", "s_cp", "s_sx", "s_xh", "s_dbg",
                 "ct", "cv", "cs"]:
        c = nc.semaphore(name)
        SEMS[name] = c.__enter__()
        live["@" + name] = c
        CNT[name] = 0

    def dinc(inst, name):
        inst.then_inc(SEMS[name], 16)
        CNT[name] += 16
        return CNT[name]

    def einc(inst, name):
        inst.then_inc(SEMS[name], 1)
        CNT[name] += 1
        return CNT[name]

    def wt(eng, name, val):
        if val > 0:
            eng.wait_ge(SEMS[name], val)

    M = {}

    def ckpt(name):
        if upto == name:
            raise _StopEmit

    with nc.Block() as blk:
      try:
          @blk.gpsimd
          def _(g):
              for fi in range(NFI):
                  dinc(g.dma_start(out=wxs_s[:, fi, :],
                                   in_=w_in[fi * 128:(fi + 1) * 128, 0:DI]), "s_wxs")
              dinc(g.dma_start(out=wdt_s[:], in_=w_dt[:]), "s_wsm")
              dinc(g.dma_start(out=wcv[:], in_=w_conv[:]), "s_wsm")
              dinc(g.dma_start(out=bdt_s[:], in_=b_dt[:]), "s_wsm")
              dinc(g.dma_start(out=g_s[:], in_=ln_gd[:]), "s_wsm")
              dinc(g.dma_start(out=bln_s[:], in_=ln_bd[:]), "s_wsm")
              dinc(g.dma_start(out=b1_s[:], in_=b1d[:]), "s_wsm")
              dinc(g.dma_start(out=rep_s[:], in_=repd[:]), "s_wsm")
              dinc(g.dma_start(out=hm_s[:], in_=hmd[:]), "s_wsm")
              M["wsm_all"] = CNT["s_wsm"]

          @blk.vector
          def _(v):
              v.memset(meanv[:], 1.0 / 2048.0)
              v.memset(cst[:, 0:1], EPS)
              v.memset(cst[:, 1:2], 1.0)
              v.memset(cst[:, 2:3], -1.0)
              v.memset(cst[:, 3:4], 0.0)
              v.memset(onesK[:], 1.0 / 16.0)
              i = v.memset(onesB[:], 1.0)
              i = v.memset(hallo[:], 0.0)
              M["hz"] = einc(i, "cv")

          for k in range(NCH):
              @blk.gpsimd
              def _(g, k=k):
                  if k >= 2:
                      wt(g, "cv", M[("waug", k - 2)])
                  dinc(g.dma_start(out=wxp_t[:, k % 2, :],
                                   in_=w_xproj[k * 128:(k + 1) * 128, :]),
                       "s_wxp%d" % (k % 2))

              @blk.vector
              def _(v, k=k):
                  wt(v, "s_wxp%d" % (k % 2), 16 * (k // 2 + 1))
                  v.drain()
                  v.reduce_sum(tmpc[:], wxp_t[:, k % 2, 64:80], axis=AX.X)
                  v.drain()
                  v.tensor_copy(waug[:, k, 80:81], tmpc[:])
                  v.tensor_copy(waug[:, k, 0:64], wxp_t[:, k % 2, 0:64])
                  v.tensor_copy(waug[:, k, 64:80], wxp_t[:, k % 2, 80:96])
                  i = v.memset(waug[:, k, 81:82], 1.0 / 2048.0)
                  M[("waug", k)] = einc(i, "cv")

          @blk.sync
          def _(s):
              wt(s, "cv", M["hz"])
              dinc(s.dma_start(out=hallo[0:66, :], in_=xhalo[:]), "s_xh")

          def emit_ln_stats(v, xsrc, p, mref):
              v.bn_stats(lnst[0:p, 0:6], xsrc[:, 0:512])
              v.bn_stats(lnst[0:p, 6:12], xsrc[:, 512:1024])
              v.drain()
              v.bn_aggr(lnmv[0:p, :], lnst[0:p, :])
              i = v.drain()
              M[mref] = einc(i, "cv")

          for m in range(NT):
              @blk.sync
              def _(s, m=m):
                  if m >= 2:
                      wt(s, "cv", M[("lnx", m - 2)])
                  dinc(s.dma_start(out=xt_l[:, m % 2, :],
                                   in_=xin[m * 128:(m + 1) * 128, :]),
                       "s_x%d" % (m % 2))

              @blk.vector
              def _(v, m=m):
                  wt(v, "s_x%d" % (m % 2), 16 * (m // 2 + 1))
                  if m >= 2:
                      wt(v, "s_tr%d" % (m % 2), M[("tr1", m - 2)])
                  emit_ln_stats(v, xt_l[:, m % 2, :], 128, ("lnst", m))

              @blk.scalar
              def _(sc, m=m):
                  wt(sc, "cv", M[("lnst", m)])
                  i = sc.activation(lnsc[:, 0:1], lnmv[:, 1:2], AF.Sqrt, bias=cst[:, 0:1])
                  M[("sq", m)] = einc(i, "cs")

              @blk.vector
              def _(v, m=m):
                  wt(v, "cs", M[("sq", m)])
                  v.reciprocal(lnsc[:, 1:2], lnsc[:, 0:1])
                  v.drain()
                  v.tensor_mul(lnsc[:, 2:3], lnmv[:, 0:1], lnsc[:, 1:2])
                  v.drain()
                  i = v.tensor_scalar(lnx_bf[:, m % 2, :], xt_l[:, m % 2, :],
                                      lnsc[:, 1:2], lnsc[:, 2:3],
                                      AL.mult, AL.subtract)
                  M[("lnx", m)] = einc(i, "cv")

              @blk.sync
              def _(s, m=m):
                  wt(s, "cv", M[("lnx", m)])
                  for fc in range(NFI):
                      dinc(s.dma_start_transpose(
                          lnxT[:, fc, m * 128:(m + 1) * 128],
                          lnx_bf[:, m % 2, fc * 128:(fc + 1) * 128]),
                          "s_tr%d" % (m % 2))
                  M[("tr1", m)] = CNT["s_tr%d" % (m % 2)]

          @blk.vector
          def _(v):
              wt(v, "s_xh", 16)
              emit_ln_stats(v, hallo[:], 80, "lnsth")

          @blk.scalar
          def _(sc):
              wt(sc, "cv", M["lnsth"])
              i = sc.activation(lnsc[0:80, 0:1], lnmv[0:80, 1:2], AF.Sqrt, bias=cst[0:80, 0:1])
              M["sqh"] = einc(i, "cs")

          @blk.vector
          def _(v):
              wt(v, "cs", M["sqh"])
              v.reciprocal(lnsc[0:80, 1:2], lnsc[0:80, 0:1])
              v.drain()
              v.tensor_mul(lnsc[0:80, 2:3], lnmv[0:80, 0:1], lnsc[0:80, 1:2])
              v.drain()
              i = v.tensor_scalar(hallo_b[:], hallo[:], lnsc[0:80, 1:2],
                                  lnsc[0:80, 2:3], AL.mult, AL.subtract)
              M["lnxh"] = einc(i, "cv")

          for fc in range(NFI):
              @blk.sync
              def _(s, fc=fc):
                  wt(s, "cv", M["lnxh"] if fc == 0 else M[("htr", fc - 1)])
                  dinc(s.dma_start_transpose(
                      htr[:], hallo_b[0:80, fc * 128:(fc + 1) * 128]), "s_tr0")
                  M[("trh", fc)] = CNT["s_tr0"]

              @blk.vector
              def _(v, fc=fc):
                  wt(v, "s_tr0", M[("trh", fc)])
                  i = v.tensor_copy(lnxT[:, fc, 2048:2114], htr[:, 0:66])
                  M[("htr", fc)] = einc(i, "cv")

          ckpt('ln1')
          for c in range(NCH):
              @blk.tensor
              def _(t, c=c):
                  if c == 0:
                      wt(t, "s_wxs", CNT["s_wxs"])
                      wt(t, "cv", M[("htr", NFI - 1)])
                  ph = PA[:, (c % 2) * 512:(c % 2) * 512 + 66]
                  if c >= 2:
                      wt(t, "cv", M[("hxe", c - 2)])
                  for fi in range(NFI):
                      i = t.matmul(ph, wxs_s[:, fi, c * 128:(c + 1) * 128],
                                   lnxT[:, fi, 2048:2114],
                                   start=(fi == 0), stop=(fi == NFI - 1))
                  M[("hmm", c)] = einc(i, "ct")

              @blk.vector
              def _(v, c=c):
                  wt(v, "ct", M[("hmm", c)])
                  ph = PA[:, (c % 2) * 512:(c % 2) * 512 + 66]
                  i = v.tensor_copy(hx[:, c, :], ph)
                  M[("hxe", c)] = einc(i, "cv")

          ckpt('a0')
          xpg = [PB[0:82, 0:512], PB[0:82, 512:1024], PC[0:82, 0:512],
                 PC[0:82, 512:1024]]
          for c in range(NCH):
              grp = grp_even

              @blk.tensor
              def _(t, c=c, grp=grp):
                  if c == 0:
                      wt(t, "s_tr0", CNT["s_tr0"])
                      wt(t, "s_tr1", CNT["s_tr1"])
                      wt(t, "cv", M[("hxe", NCH - 1)])
                  else:
                      wt(t, "cs", M[("evac", c - 1)])
                  for fi in range(NFI):
                      for s4 in range(NSB):
                          i = t.matmul(grp[s4],
                                       wxs_s[:, fi, c * 128:(c + 1) * 128],
                                       lnxT[:, fi, s4 * 512:(s4 + 1) * 512],
                                       start=(fi == 0), stop=(fi == NFI - 1))
                  M[("xzmm", c)] = einc(i, "ct")

              @blk.tensor
              def _(t, c=c):
                  if c == 0:
                      return
                  cc = c - 1
                  wt(t, "cs", M[("xs", cc)])
                  wt(t, "cv", M[("waug", cc)])
                  for b in range(1, len(BLOCKS)):
                      BW, BOFF = BLOCKS[b]
                      i = t.matmul(xpg[b - 1], waug[:, cc, :],
                                   xs_ch[:, cc % 2, BOFF:BOFF + BW],
                                   start=(cc == 0), stop=(cc == NCH - 1))
                  M[("xpc", cc)] = einc(i, "ct")

              @blk.vector
              def _(v, c=c):
                  if c == 0:
                      wt(v, "s_wsm", M["wsm_all"])
                  if c >= 2:
                      wt(v, "cs", M[("p1", c - 2)])
                  v.tensor_copy(xz_s[:, c % 2, 0:65], hx[:, c, 0:65])
                  i = v.tensor_copy(xz_s[:, c % 2, 65 + T:66 + T], hx[:, c, 65:66])
                  M[("hcp", c)] = einc(i, "cv")

              @blk.scalar
              def _(sc, c=c, grp=grp):
                  if c == 0:
                      wt(sc, "s_wsm", M["wsm_all"])
                  wt(sc, "ct", M[("xzmm", c)])
                  if c >= 2:
                      wt(sc, "cv", M[("conv", c - 2)])
                      wt(sc, "ct", M[("xpc", c - 2)])
                  for s4 in range(NSB):
                      i = sc.activation(xz_s[:, c % 2, 65 + s4 * 512:65 + (s4 + 1) * 512],
                                        grp[s4], AF.Copy)
                  M[("evac", c)] = einc(i, "cs")
                  wt(sc, "cv", M[("hcp", c)])
                  sc.drain()
                  i = sc.activation(xc_s[:], xz_s[:, c % 2, 1:1 + TH],
                                    AF.Copy, scale=wcv[:, c, 1:2])
                  M[("p1", c)] = einc(i, "cs")

              @blk.vector
              def _(v, c=c):
                  wt(v, "cs", M[("p1", c)])
                  v.drain()
                  v.scalar_tensor_tensor(xc_s[:], xz_s[:, c % 2, 0:TH],
                                         wcv[:, c, 0:1], xc_s[:],
                                         AL.mult, AL.add)
                  v.drain()
                  i = v.scalar_tensor_tensor(xc_s[:], xz_s[:, c % 2, 2:2 + TH],
                                             wcv[:, c, 2:3], xc_s[:],
                                             AL.mult, AL.add)
                  M[("conv", c)] = einc(i, "cv")

              @blk.scalar
              def _(sc, c=c):
                  wt(sc, "cv", M[("conv", c)])
                  sc.activation(xs_ch[:, c % 2, :], xc_s[:], AF.Silu)
                  sc.drain()
                  i = sc.activation(xs_bh[:, c, :], xs_ch[:, c % 2, 0:HW], AF.Copy)
                  M[("xs", c)] = einc(i, "cs")

          ckpt('a1')

          @blk.tensor
          def _(t):
              cc = NCH - 1
              wt(t, "cs", M[("xs", cc)])
              wt(t, "cv", M[("waug", cc)])
              for b in range(1, len(BLOCKS)):
                  BW, BOFF = BLOCKS[b]
                  i = t.matmul(xpg[b - 1], waug[:, cc, :],
                               xs_ch[:, cc % 2, BOFF:BOFF + BW],
                               start=(cc == 0), stop=(cc == NCH - 1))
              M[("xpc", cc)] = einc(i, "ct")

          sclose("xc_s", "xz_s", "wxs_s", "htr", "hallo_b", "hallo", "lnx_bf",
                 "xt_l", "wxp_t")
          wz_s = sopen("wz_s", [128, NFI, DI], BF16)
          rows_a = sopen("rows_a", [1, 3 * TH], F32)
          dtbf = sopen("dtbf", [128, NCH, SB], BF16)
          wx2 = sopen("wx2", [64, len(BLOCKS), SB], BF16)
          cpx = sopen("cpx", [128, len(BLOCKS), SB], F32)
          dtw = sopen("dtw", [128, 18, SB], F32)

          @blk.gpsimd
          def _(g):
              wt(g, "cs", M[("xs", NCH - 1)])
              for fi in range(NFI):
                  if fi >= 1:
                      wt(g, "s_wz", M[("wzc", fi - 1)])
                  M[("wzc", fi)] = dinc(
                      g.dma_start(out=wz_s[:, fi, :],
                                  in_=w_in[fi * 128:(fi + 1) * 128, DI:2 * DI]), "s_wz")
          rowsall = sopen("rowsall", [1, 4096], F32, side="right")
          n16t = sopen("n16t", [16, SB], F32, side="right")
          Cp = sopen("Cp", [16, T], F32, side="right")

          dt_rot = [PAs[1], PAs[2], PAs[3]]
          NB = len(BLOCKS)

          @blk.tensor
          def _(t):
              wt(t, "cs", M[("xs", NCH - 1)])
              wt(t, "cv", M[("waug", NCH - 1)])
              for k in range(NCH):
                  i = t.matmul(PA[0:82, 0:HW], waug[:, k, :],
                               xs_bh[:, k, :],
                               start=(k == 0), stop=(k == NCH - 1))
              M[("xph", 0)] = einc(i, "ct")

          for sb, (BW, BOFF) in enumerate(BLOCKS):
              @blk.scalar
              def _(sc, sb=sb, BW=BW):
                  if sb == 0:
                      wt(sc, "ct", M[("xph", 0)])
                  elif sb == 1:
                      wt(sc, "ct", M[("xpc", NCH - 1)])
                  pg = PA if sb == 0 else \
                       (PB if sb <= 2 else PC)
                  co = 0 if sb == 0 else (0 if sb % 2 == 1 else 512)
                  sc.activation(wx2[0:64, sb, 0:BW], pg[0:64, co:co + BW], AF.Copy)
                  i = sc.activation(cpx[64:82, sb, 0:BW], pg[64:82, co:co + BW],
                                    AF.Copy)
                  M[("xpe", sb)] = einc(i, "cs")

              @blk.gpsimd
              def _(g, sb=sb, BW=BW, BOFF=BOFF):
                  wt(g, "cs", M[("xpe", sb)])
                  if sb > 0:
                      dinc(g.dma_start(out=Cp[:, BOFF - 64:BOFF - 64 + BW],
                                       in_=cpx[64:80, sb, 0:BW]), "s_cp")
                  dinc(g.dma_start(
                      out=rows_a[0:1, R_SUMBP + BOFF:R_SUMBP + BOFF + BW],
                      in_=cpx[80:81, sb, 0:BW]), "s_sx")
                  dinc(g.dma_start(
                      out=rows_a[0:1, R_XM + BOFF:R_XM + BOFF + BW],
                      in_=cpx[81:82, sb, 0:BW]), "s_sx")

          ckpt("xpe")

          def emit_dtm(b, h):
              BW, BOFF = BLOCKS[b]

              @blk.tensor
              def _(t, b=b, h=h, BW=BW):
                  if h == 0:
                      if b == 0:
                          wt(t, "cs", M[("xpe", NB - 1)])
                      else:
                          wt(t, "cs", M[("dtmrow", b - 1)])
                  for mc in range(8 * h, 8 * h + 8):
                      wt(t, "cv", M[("dtbf", b, mc)])
                      i = t.matmul(PC[0:1, 512:512 + BW], meanv[:], dtbf[:, mc, 0:BW],
                                   start=(mc == 0), stop=(mc == NCH - 1))
                      M[("dtmm", b, mc)] = einc(i, "ct")
                  if h == 1:
                      M[("dtm", b)] = CNT["ct"]

              if h == 1:
                  @blk.scalar
                  def _(sc, b=b, BW=BW, BOFF=BOFF):
                      wt(sc, "ct", M[("dtm", b)])
                      i = sc.activation(rows_a[0:1, R_DTM + BOFF:R_DTM + BOFF + BW],
                                        PC[0:1, 512:512 + BW], AF.Copy)
                      M[("dtmrow", b)] = einc(i, "cs")

          for sb, (BW, BOFF) in enumerate(BLOCKS):
              if sb > 0:
                  emit_dtm(sb - 1, 0)

              for mc in range(NCH):
                  ckpt("mc%d_%d" % (sb, mc))
                  if sb > 0 and mc == 8:
                      emit_dtm(sb - 1, 1)

                  @blk.tensor
                  def _(t, sb=sb, mc=mc, BW=BW):
                      grp = dt_rot[mc % 3]
                      lk = (sb, mc - 3) if mc >= 3 else \
                           ((sb - 1, [15, 13, 14][mc]) if sb > 0 else None)
                      if lk is not None:
                          wt(t, "cs", M[("dty",) + lk])
                      if mc == 0:
                          wt(t, "cs", M[("xpe", sb)])
                      i = t.matmul(grp[:, 0:BW], wdt_s[:, mc * 128:(mc + 1) * 128],
                                   wx2[0:64, sb, 0:BW], start=True, stop=True)
                      M[("dt", sb, mc)] = einc(i, "ct")

                  @blk.scalar
                  def _(sc, sb=sb, mc=mc, BW=BW):
                      yr, ar, er = mc % 8, 8 + mc % 2, 10 + mc % 8
                      grp = dt_rot[mc % 3]
                      wt(sc, "ct", M[("dt", sb, mc)])
                      lk = (sb, mc - 8) if mc >= 8 else \
                           ((sb - 1, mc + 8) if sb > 0 else None)
                      if lk is not None:
                          wt(sc, "cv", M[("dtbf",) + lk])
                      i = sc.activation(dtw[:, yr, 0:BW], grp[:, 0:BW], AF.Identity,
                                        bias=bdt_s[:, mc:mc + 1])
                      M[("dty", sb, mc)] = einc(i, "cs")
                      sc.drain()
                      sc.activation(dtw[:, ar, 0:BW], dtw[:, yr, 0:BW], AF.Abs)
                      sc.drain()
                      i = sc.activation(dtw[:, er, 0:BW], dtw[:, ar, 0:BW],
                                        AF.Exp, scale=cst[:, 2:3])
                      M[("dtexp", sb, mc)] = einc(i, "cs")

                  @blk.vector
                  def _(v, sb=sb, mc=mc, BW=BW):
                      er = 10 + mc % 8
                      wt(v, "cs", M[("dtexp", sb, mc)])
                      v.drain()
                      i = v.tensor_single_scalar(dtw[:, er, 0:BW], dtw[:, er, 0:BW],
                                                 1.0, AL.add)
                      M[("dtep1", sb, mc)] = einc(i, "cv")

                  if mc % 8 == 7:
                      @blk.scalar
                      def _(sc, sb=sb, mc=mc, BW=BW):
                          for q in range(mc - 7, mc + 1):
                              er = 10 + q % 8
                              wt(sc, "cv", M[("dtep1", sb, q)])
                              i = sc.activation(dtw[:, er, 0:BW],
                                                dtw[:, er, 0:BW], AF.Ln)
                              M[("dtln", sb, q)] = einc(i, "cs")

                      @blk.vector
                      def _(v, sb=sb, mc=mc, BW=BW):
                          for q in range(mc - 7, mc + 1):
                              yr, er = q % 8, 10 + q % 8
                              wt(v, "cs", M[("dtln", sb, q)])
                              if sb > 0:
                                  wt(v, "ct", M[("dtmm", sb - 1, q)])
                              v.drain()
                              i = v.scalar_tensor_tensor(
                                  dtbf[:, q, 0:BW], dtw[:, yr, 0:BW],
                                  cst[:, 3:4], dtw[:, er, 0:BW],
                                  AL.max, AL.add)
                              M[("dtbf", sb, q)] = einc(i, "cv")

              ckpt("dt%d" % sb)

          emit_dtm(NB - 1, 0)
          emit_dtm(NB - 1, 1)

          @blk.scalar
          def _(sc):
              sc.drain()
              i = sc.activation(rows_a[0:1, R_DTM:R_DTM + TH],
                                rows_a[0:1, R_DTM:R_DTM + TH], AF.Exp, scale=cst[0:1, 2:3])
              M["decay"] = einc(i, "cs")

          @blk.vector
          def _(v):
              wt(v, "s_sx", CNT["s_sx"])
              wt(v, "s_wsm", M["wsm_all"])
              v.tensor_mul(rows_a[0:1, R_SUMBP:R_SUMBP + TH],
                           rows_a[0:1, R_SUMBP:R_SUMBP + TH],
                           rows_a[0:1, R_XM:R_XM + TH])
              v.drain()
              v.tensor_mul(rows_a[0:1, R_SUMBP:R_SUMBP + HW],
                           rows_a[0:1, R_SUMBP:R_SUMBP + HW], hm_s[:])
              wt(v, "cs", M["decay"])
              v.drain()
              v.tensor_tensor_scan(rows_a[0:1, R_XM:R_XM + TH],
                                   rows_a[0:1, R_DTM:R_DTM + TH],
                                   rows_a[0:1, R_SUMBP:R_SUMBP + TH],
                                   0.0, AL.mult, AL.add)
              v.drain()
              i = v.tensor_copy(rowsall[0:1, 2048:4096],
                                rows_a[0:1, R_XM + HW:R_XM + HW + T])
              M["scan"] = einc(i, "cv")

          sclose("dtw", "cpx", "wx2", "dtbf")
          z_a = sopen("z_a", [128, NCH, 1024], BF16, side="right")
          z_b = sopen("z_b", [128, NCH, 1024], BF16, side="right")

          zslots = [PAs[0], PAs[1], PAs[2], PAs[3], PB[:, 0:512], PB[:, 512:1024],
                    PC[:, 0:512]]

          for c in range(NCH):
              @blk.tensor
              def _(t, c=c):
                  if c == 0:
                      wt(t, "cs", M[("dty", NB - 1, NCH - 1)])
                  for fi in range(NFI):
                      if c == 0:
                          wt(t, "s_wz", M[("wzc", fi)])
                      for s4 in range(NSB):
                          flat = 4 * c + s4
                          if flat >= 7:
                              wt(t, "cs", M[("zevac", (flat - 7) // 4)])
                          i = t.matmul(zslots[flat % 7], wz_s[:, fi, c * 128:(c + 1) * 128],
                                       lnxT[:, fi, s4 * 512:(s4 + 1) * 512],
                                       start=(fi == 0), stop=(fi == NFI - 1))
                  M[("zmm", c)] = einc(i, "ct")

              @blk.scalar
              def _(sc, c=c):
                  wt(sc, "ct", M[("zmm", c)])
                  if c == 0:
                      wt(sc, "s_sx", CNT["s_sx"])
                      wt(sc, "s_cp", CNT["s_cp"])
                  for s4 in range(NSB):
                      flat = 4 * c + s4
                      ztgt = z_a if s4 < 2 else z_b
                      i = sc.activation(ztgt[:, c, (s4 % 2) * 512:(s4 % 2 + 1) * 512],
                                        zslots[flat % 7], AF.Silu)
                  M[("zevac", c)] = einc(i, "cs")

          if debug:
              @blk.gpsimd
              def _(g):
                  wt(g, "cv", M["scan"])
                  wt(g, "s_cp", CNT["s_cp"])
                  dinc(g.dma_start(out=dbg[0:1, :],
                                   in_=rows_a[0:1, R_DTM + HW:R_DTM + HW + T]), "s_dbg")
                  dinc(g.dma_start(out=dbg[1:2, :],
                                   in_=rows_a[0:1, R_SUMBP + HW:R_SUMBP + HW + T]), "s_dbg")
                  dinc(g.dma_start(out=dbg[2:3, :], in_=rowsall[0:1, 2048:4096]), "s_dbg")
                  dinc(g.dma_start(out=dbg[3:4, :], in_=Cp[0:1, :]), "s_dbg")
                  M["dbg1"] = CNT["s_dbg"]

          ckpt('scan')
          sclose("rows_a", "wz_s", "xs_bh", "xs_ch", "waug", "wdt_s", "hx",
                 "lnxT")
          wout_s = sopen("wout_s", [128, NCH, D], BF16, side="right")
          rowsb = sopen("rowsb", [1, 4096], F32)

          @blk.gpsimd
          def _(g):
              if debug:
                  wt(g, "s_dbg", M["dbg1"])
              wt(g, "cv", M["scan"])
              wt(g, "ct", M[("dtm", NB - 1)])
              prev = None
              for k in list(range(1, NCH)) + [0]:
                  if prev is not None:
                      wt(g, "s_wout", M[("woc", prev)])
                  if k == 0:
                      wt(g, "ct", M[("zmm", NCH - 1)])
                      wt(g, "s_wz", CNT["s_wz"])
                  M[("woc", k)] = dinc(
                      g.dma_start(out=wout_s[:, k, :],
                                  in_=w_outp[k * 128:(k + 1) * 128, :]), "s_wout")
                  prev = k

          mcp2g = [PB[0:1, 0:512], PB[0:1, 512:1024], PC[0:1, 0:512], PC[0:1, 512:1024]]

          for k in range(NSB):
              @blk.vector
              def _(v, k=k):
                  if k == 0:
                      wt(v, "cv", M["scan"])
                      wt(v, "s_cp", CNT["s_cp"])
                  else:
                      wt(v, "ct", M[("chmm", k - 1)])
                  v.drain()
                  i = v.tensor_mul(n16t[:], Cp[:, k * 512:(k + 1) * 512],
                                   Cp[:, k * 512:(k + 1) * 512])
                  M[("cp2", k)] = einc(i, "cv")

              @blk.tensor
              def _(t, k=k):
                  if k == 0:
                      wt(t, "cs", M[("zevac", NCH - 1)])
                      wt(t, "cv", M["scan"])
                  wt(t, "cv", M[("cp2", k)])
                  t.matmul(PA[0:1, k * 512:(k + 1) * 512], onesK[:],
                           Cp[:, k * 512:(k + 1) * 512], start=True, stop=True)
                  i = t.matmul(mcp2g[k], onesK[:], n16t[:], start=True, stop=True)
                  M[("chmm", k)] = einc(i, "ct")

          @blk.scalar
          def _(sc):
              wt(sc, "ct", M[("chmm", NSB - 1)])
              for k in range(NSB):
                  sc.activation(rowsall[0:1, k * 512:(k + 1) * 512],
                                PA[0:1, k * 512:(k + 1) * 512], AF.Copy)
                  i = sc.activation(rowsb[0:1, k * 512:(k + 1) * 512],
                                    mcp2g[k], AF.Copy)
              M["chev"] = einc(i, "cs")

          @blk.vector
          def _(v):
              wt(v, "cs", M["chev"])
              v.tensor_mul(rowsb[0:1, 2048:4096], rowsall[0:1, 0:2048],
                           rowsall[0:1, 0:2048])
              v.drain()
              v.tensor_sub(rowsb[0:1, 0:2048], rowsb[0:1, 0:2048],
                           rowsb[0:1, 2048:4096])
              v.drain()
              v.tensor_mul(rowsb[0:1, 2048:4096], rowsall[0:1, 2048:4096], rowsall[0:1, 2048:4096])
              v.drain()
              i = v.tensor_mul(rowsb[0:1, 0:2048], rowsb[0:1, 2048:4096],
                               rowsb[0:1, 0:2048])
              M["chq"] = einc(i, "cv")

          @blk.scalar
          def _(sc):
              wt(sc, "cv", M["chq"])
              i = sc.activation(rowsb[0:1, 2048:4096], rowsb[0:1, 0:2048],
                                AF.Sqrt, bias=cst[0:1, 0:1])
              M["chsq"] = einc(i, "cs")

          @blk.vector
          def _(v):
              wt(v, "cs", M["chsq"])
              if debug:
                  wt(v, "s_dbg", M["dbg1"])
              v.reciprocal(rowsb[0:1, 0:2048], rowsb[0:1, 2048:4096])
              v.drain()
              i = v.tensor_mul(rowsall[0:1, 2048:4096], rowsall[0:1, 2048:4096],
                               rowsb[0:1, 0:2048])
              M["chsr"] = einc(i, "cv")

          ckpt('a2')
          sclose("rowsb")
          lnx2T = sopen("lnx2T", [128, NFI, T], BF16)
          gt = sopen("gt", [128, 2, SB], F32)
          y_g = sopen("y_g", [128, NCH, SB], BF16)
          xt_b = sopen("xt_b", [128, 2, D], F32)
          x1t = sopen("x1t", [128, 2, D], F32)
          lnx2_bf = sopen("lnx2_bf", [128, 2, D], BF16)

          for sb in range(NSB):
              zt = z_a if sb < 2 else z_b

              @blk.tensor
              def _(t, sb=sb):
                  wt(t, "cv", M["chsr"])
                  if sb == 0:
                      wt(t, "s_wsm", M["wsm_all"])
                      wt(t, "cs", M[("zevac", NCH - 1)])
                  if sb > 0:
                      wt(t, "cv", M[("n16", sb - 1)])
                  t.matmul(PA[0:16, 1024:1536], onesB[:],
                           rowsall[0:1, sb * 512:(sb + 1) * 512], start=True, stop=True)
                  i = t.matmul(PA[0:16, 1536:2048], onesB[:],
                               rowsall[0:1, 2048 + sb * 512:2048 + (sb + 1) * 512],
                               start=True, stop=True)
                  M[("bc", sb)] = einc(i, "ct")

              @blk.vector
              def _(v, sb=sb):
                  wt(v, "ct", M[("bc", sb)])
                  v.tensor_sub(n16t[:], Cp[:, sb * 512:(sb + 1) * 512],
                               PA[0:16, 1024:1536])
                  v.drain()
                  i = v.tensor_mul(n16t[:], n16t[:], PA[0:16, 1536:2048])
                  M[("n16", sb)] = einc(i, "cv")

              @blk.tensor
              def _(t, sb=sb):
                  wt(t, "cv", M[("n16", sb)])
                  if sb > 0:
                      wt(t, "cv", M[("gate", sb - 1, NCH - 1)])
                  i = t.matmul(PB[:, 0:512], rep_s[:], n16t[:], start=True, stop=True)
                  M[("tiled", sb)] = einc(i, "ct")

              for c in range(NCH):
                  @blk.scalar
                  def _(sc, sb=sb, c=c):
                      if c == 0:
                          wt(sc, "ct", M[("tiled", sb)])
                      if c >= 2:
                          wt(sc, "cv", M[("gate", sb, c - 2)])
                      elif sb > 0:
                          wt(sc, "cv", M[("gate", sb - 1, NCH - 2 + c)])
                      i = sc.activation(gt[:, c % 2, :], PB[:, 0:512], AF.Identity,
                                        bias=bln_s[:, c:c + 1],
                                        scale=g_s[:, c:c + 1])
                      M[("gp1", sb, c)] = einc(i, "cs")

                  @blk.vector
                  def _(v, sb=sb, c=c, zt=zt):
                      if c == 0 and sb > 0:
                          wt(v, "ct", M[("wo", sb - 1, 3)])
                      wt(v, "cs", M[("gp1", sb, c)])
                      wt(v, "cs", M[("zevac", c)])
                      v.drain()
                      i = v.tensor_mul(y_g[:, c, :], gt[:, c % 2, :],
                                       zt[:, c, (sb % 2) * 512:(sb % 2 + 1) * 512])
                      M[("gate", sb, c)] = einc(i, "cv")

              for tb in range(4):
                  m = sb * 4 + tb
                  ptok = PC if tb % 2 == 0 else PB

                  @blk.tensor
                  def _(t, sb=sb, tb=tb, ptok=ptok):
                      if tb == 1:
                          wt(t, "cs", M[("gp1", sb, NCH - 1)])
                      if tb >= 2:
                          wt(t, "cv", M[("x1", sb, tb - 2)])
                      for c in range(NCH):
                          wt(t, "cv", M[("gate", sb, c)])
                          if sb == 0 and tb == 0:
                              wt(t, "s_wout", M[("woc", c)])
                          t.matmul(ptok[:, 0:512],
                                   y_g[:, c, tb * 128:(tb + 1) * 128],
                                   wout_s[:, c, 0:512],
                                   start=(c == 0), stop=(c == NCH - 1))
                          i = t.matmul(ptok[:, 512:1024],
                                       y_g[:, c, tb * 128:(tb + 1) * 128],
                                       wout_s[:, c, 512:1024],
                                       start=(c == 0), stop=(c == NCH - 1))
                      M[("wo", sb, tb)] = einc(i, "ct")

                  @blk.sync
                  def _(s, m=m):
                      if m == 0:
                          wt(s, "cs", M[("zevac", NCH - 1)])
                      if m >= 2:
                          wt(s, "cv", M[("lnx2", m - 2)])
                      dinc(s.dma_start(out=xt_b[:, m % 2, :],
                                       in_=xin[m * 128:(m + 1) * 128, :]),
                           "s_xb%d" % (m % 2))

                  @blk.vector
                  def _(v, sb=sb, tb=tb, m=m, ptok=ptok):
                      wt(v, "ct", M[("wo", sb, tb)])
                      wt(v, "s_xb%d" % (m % 2), 16 * (m // 2 + 1))
                      if m >= 2:
                          wt(v, "s_st%d" % (m % 2), M[("spill", m - 2)])
                          wt(v, "s_t2%d" % (m % 2), M[("tr2", m - 2)])
                      i = v.tensor_add(x1t[:, m % 2, :], ptok[:], xt_b[:, m % 2, :])
                      M[("x1", sb, tb)] = einc(i, "cv")
                      v.drain()
                      emit_ln_stats(v, x1t[:, m % 2, :], 128, ("lnst2", m))

                  @blk.scalar
                  def _(sc, m=m):
                      wt(sc, "cv", M[("lnst2", m)])
                      i = sc.activation(lnsc[:, 0:1], lnmv[:, 1:2], AF.Sqrt, bias=cst[:, 0:1])
                      M[("sq2", m)] = einc(i, "cs")

                  @blk.vector
                  def _(v, m=m):
                      wt(v, "cs", M[("sq2", m)])
                      v.reciprocal(lnsc[:, 1:2], lnsc[:, 0:1])
                      v.drain()
                      v.tensor_mul(lnsc[:, 2:3], lnmv[:, 0:1], lnsc[:, 1:2])
                      v.drain()
                      i = v.tensor_scalar(lnx2_bf[:, m % 2, :], x1t[:, m % 2, :],
                                          lnsc[:, 1:2], lnsc[:, 2:3],
                                          AL.mult, AL.subtract)
                      M[("lnx2", m)] = einc(i, "cv")

                  @blk.sync
                  def _(s, sb=sb, tb=tb, m=m):
                      wt(s, "cv", M[("x1", sb, tb)])
                      M[("spill", m)] = dinc(
                          s.dma_start(out=out[m * 128:(m + 1) * 128, :],
                                      in_=x1t[:, m % 2, :]), "s_st%d" % (m % 2))
                      wt(s, "cv", M[("lnx2", m)])
                      for fc in range(NFI):
                          dinc(s.dma_start_transpose(
                              lnx2T[:, fc, m * 128:(m + 1) * 128],
                              lnx2_bf[:, m % 2, fc * 128:(fc + 1) * 128]),
                              "s_t2%d" % (m % 2))
                      M[("tr2", m)] = CNT["s_t2%d" % (m % 2)]

          ckpt('b1')
          sclose("lnx2_bf", "x1t", "xt_b", "y_g", "gt")
          sclose("wout_s", "z_b", "z_a", "Cp", "n16t", "rowsall")
          w1_s = sopen("w1_s", [128, NFI, 4096], BF16)
          h_sb = sopen("h_sb", [128, NIO, SB], BF16)
          w2buf = sopen("w2buf", [128, 3, 8, D], BF16)
          xpt = sopen("xpt", [128, 2, D], F32)
          ot = sopen("ot", [128, 2, D], F32)
          b2_s = sopen("b2_s", [128, D], F32)
          w2r = w2d.rearrange("(j p) d -> p j d", p=128)

          w1sems = ["s_w1", "s_wxs", "s_wxp0", "s_wxp1"]

          @blk.gpsimd
          def _(g):
              wt(g, "cv", M[("lnx2", NT - 1)])
              wt(g, "s_st0", CNT["s_st0"])
              wt(g, "s_st1", CNT["s_st1"])
              wt(g, "s_t20", CNT["s_t20"])
              wt(g, "s_t21", CNT["s_t21"])
              M["b2"] = dinc(g.dma_start(out=b2_s[:], in_=b2d[:]), "s_wsm")
              w1r = w1d.rearrange("(k p) d -> p k d", p=128)
              for pn in range(8):
                  sem = w1sems[pn % 4]
                  if pn >= 4:
                      wt(g, sem, M[("w1p", pn - 4)])
                  M[("w1p", pn)] = dinc(
                      g.dma_start(out=w1_s[:, :, pn * 512:(pn + 1) * 512],
                                  in_=w1r[:, :, pn * 512:(pn + 1) * 512]), sem)

          for sb in range(NSB):
              for io in range(NIO):
                  @blk.tensor
                  def _(t, sb=sb, io=io):
                      if sb == 0 and io == 0:
                          wt(t, "s_t20", CNT["s_t20"])
                          wt(t, "s_t21", CNT["s_t21"])
                          wt(t, "cv", M[("x1", NSB - 1, 3)])
                      if sb == 0 and io % 4 == 0:
                          wt(t, w1sems[(io // 4) % 4], M[("w1p", io // 4)])
                      grp = PAs[io % 4]
                      fkey = ("hfree", sb, io - 4) if io >= 4 else \
                             (("hfree", sb - 1, io + NIO - 4) if sb > 0 else None)
                      if fkey is not None:
                          wt(t, "cs", M[fkey])
                      for fi in range(NFI):
                          i = t.matmul(grp, w1_s[:, fi, io * 128:(io + 1) * 128],
                                       lnx2T[:, fi, sb * 512:(sb + 1) * 512],
                                       start=(fi == 0), stop=(fi == NFI - 1))
                      M[("w1", sb, io)] = einc(i, "ct")

                  @blk.scalar
                  def _(sc, sb=sb, io=io):
                      wt(sc, "ct", M[("w1", sb, io)])
                      i = sc.activation(h_sb[:, io, :], PAs[io % 4], AF.Relu,
                                        bias=b1_s[:, io:io + 1])
                      M[("hfree", sb, io)] = einc(i, "cs")

              for p2 in range(2):
                  tbs = (0, 1) if p2 == 0 else (2, 3)

                  for qh in range(2):
                      @blk.gpsimd
                      def _(g, sb=sb, p2=p2, qh=qh):
                          for q in (2 * qh, 2 * qh + 1):
                              qi = sb * 8 + p2 * 4 + q
                              if qi >= 3:
                                  wt(g, "ct", M[("w2used", qi - 3)])
                              M[("w2ld", qi)] = dinc(
                                  g.dma_start(out=w2buf[:, qi % 3, :, :],
                                              in_=w2r[:, q * 8:(q + 1) * 8, :]),
                                  "s_w2q%d" % (qi % 3))

                      @blk.tensor
                      def _(t, sb=sb, p2=p2, qh=qh, tbs=tbs):
                          if qh == 0 and not (sb == 0 and p2 == 0):
                              pm = (sb * 4 + 1) if p2 == 1 else ((sb - 1) * 4 + 3)
                              wt(t, "cv", M[("outv", pm)])
                          for q in (2 * qh, 2 * qh + 1):
                              qi = sb * 8 + p2 * 4 + q
                              wt(t, "s_w2q%d" % (qi % 3), M[("w2ld", qi)])
                              for j in range(8):
                                  io = q * 8 + j
                                  wt(t, "cs", M[("hfree", sb, io)])
                                  for tbi, tb in enumerate(tbs):
                                      pt = PC if tbi == 0 else PB
                                      t.matmul(pt[:, 0:512],
                                               h_sb[:, io, tb * 128:(tb + 1) * 128],
                                               w2buf[:, qi % 3, j, 0:512],
                                               start=(io == 0), stop=(io == NIO - 1))
                                      i = t.matmul(pt[:, 512:1024],
                                                   h_sb[:, io, tb * 128:(tb + 1) * 128],
                                                   w2buf[:, qi % 3, j, 512:1024],
                                                   start=(io == 0), stop=(io == NIO - 1))
                              M[("w2used", qi)] = einc(i, "ct")
                          if qh == 1:
                              M[("w2", sb, p2)] = CNT["ct"]

                  for tbi, tb in enumerate(tbs):
                      m = sb * 4 + tb
                      pt = PC if tbi == 0 else PB

                      @blk.sync
                      def _(s, m=m):
                          if m == 0:
                              wt(s, "s_st0", CNT["s_st0"])
                              wt(s, "s_st1", CNT["s_st1"])
                          if m >= 2:
                              wt(s, "cv", M[("outv", m - 2)])
                          dinc(s.dma_start(out=xpt[:, m % 2, :],
                                           in_=out[m * 128:(m + 1) * 128, :]),
                               "s_xp%d" % (m % 2))

                      @blk.vector
                      def _(v, sb=sb, p2=p2, m=m, pt=pt):
                          wt(v, "ct", M[("w2", sb, p2)])
                          wt(v, "s_xp%d" % (m % 2), 16 * (m // 2 + 1))
                          if m == 0:
                              wt(v, "s_wsm", M["b2"])
                          if m >= 2:
                              wt(v, "s_st%d" % (m % 2), M[("outd", m - 2)])
                          v.tensor_add(ot[:, m % 2, :], pt[:], b2_s[:])
                          v.drain()
                          i = v.tensor_add(ot[:, m % 2, :], ot[:, m % 2, :],
                                           xpt[:, m % 2, :])
                          M[("outv", m)] = einc(i, "cv")

                      @blk.sync
                      def _(s, m=m):
                          wt(s, "cv", M[("outv", m)])
                          M[("outd", m)] = dinc(
                              s.dma_start(out=out[m * 128:(m + 1) * 128, :],
                                          in_=ot[:, m % 2, :]), "s_st%d" % (m % 2))

          @blk.sync
          def _(s):
              s.wait_ge(SEMS["s_st0"], CNT["s_st0"])
              s.wait_ge(SEMS["s_st1"], CNT["s_st1"])

      except _StopEmit:
        pass
    for n in reversed(list(live)):
        live.pop(n).__exit__(None, None, None)
    return nc


def _prep_inputs(inputs):
    X = np.ascontiguousarray(
        np.asarray(inputs["L0_attn_x"], np.float32).reshape(2 * 8192, 1024))
    W_in = np.ascontiguousarray(np.asarray(inputs["L0_attn_W_in"], np.float32))
    W_conv = np.asarray(inputs["L0_attn_W_conv"], np.float32)[:, 0, :]
    wcv = np.ascontiguousarray(
        W_conv.reshape(NCH, 128, 3).transpose(1, 0, 2).reshape(128, NCH * 3))
    W_xproj = np.ascontiguousarray(np.asarray(inputs["L0_attn_W_xproj"], np.float32))
    W_dt = np.ascontiguousarray(np.asarray(inputs["L0_attn_W_dt"], np.float32))
    b_dt = np.ascontiguousarray(
        np.asarray(inputs["L0_attn_b_dt"], np.float32).reshape(NCH, 128).T)
    W_out = np.ascontiguousarray(np.asarray(inputs["L0_attn_W_out"], np.float32))
    ln_g = np.ascontiguousarray(
        np.asarray(inputs["L0_attn_ln_g"], np.float32).reshape(NCH, 128).T)
    ln_b = np.ascontiguousarray(
        np.asarray(inputs["L0_attn_ln_b"], np.float32).reshape(NCH, 128).T)
    W1 = np.ascontiguousarray(np.asarray(inputs["L0_ffn_W1"], np.float32))
    b1 = np.ascontiguousarray(
        np.asarray(inputs["L0_ffn_b1"], np.float32).reshape(NIO, 128).T)
    W2 = np.ascontiguousarray(np.asarray(inputs["L0_ffn_W2"], np.float32))
    b2 = np.ascontiguousarray(np.broadcast_to(
        np.asarray(inputs["L0_ffn_b2"], np.float32)[None, :], (128, 1024)).copy())
    rep = np.zeros((16, 128), np.float32)
    for s in range(16):
        rep[s, s::16] = 1.0

    per_batch = 8192 // T
    in_maps = []
    for c in range(8):
        t0 = c * T
        halo = np.zeros((66, 1024), np.float32)
        if (c % per_batch) != 0:
            halo[0:65] = X[t0 - 65:t0]
        if (c % per_batch) != per_batch - 1:
            halo[65] = X[t0 + T]
        hmask = np.full((1, 64), 0.0 if (c % per_batch) == 0 else 1.0, np.float32)
        in_maps.append({
            "xin": np.ascontiguousarray(X[t0:t0 + T]),
            "xhalo": halo,
            "hmask": hmask,
            "w_in": W_in, "w_conv": wcv, "w_xproj": W_xproj, "w_dt": W_dt,
            "b_dt": b_dt, "w_outp": W_out, "ln_g": ln_g, "ln_b": ln_b,
            "w1": W1, "b1": b1, "w2": W2, "b2bc": b2, "repmat": rep,
        })
    return in_maps


def kernel(**inputs):
    in_maps = _prep_inputs(inputs)
    nc = build_nc(debug=False)
    res = run_bass_kernel_spmd(nc, in_maps, core_ids=list(range(8)))
    outs = [np.asarray(r["out"]) for r in res.results]
    return np.concatenate(outs, axis=0).reshape(2, 8192, 1024).astype(np.float32)
